# version 40
# baseline (speedup 1.0000x reference)
import contextlib
import numpy as np
import ml_dtypes
import concourse.bass as bass
import concourse.mybir as mybir
from concourse.bass_utils import run_bass_kernel_spmd

F32 = mybir.dt.float32
BF16 = mybir.dt.bfloat16
I16 = mybir.dt.int16
ALU = mybir.AluOpType
AF = mybir.ActivationFunctionType
AX = mybir.AxisListType.X

P = 128
CFG = dict(N=4096, D=1024, NH=16, FI=2816, SL=512)


def build(cfg):
    N, D, NH, FI, SL = cfg["N"], cfg["D"], cfg["NH"], cfg["FI"], cfg["SL"]
    T = N // 2
    KC = D // P
    NSL = N // SL
    OSL = T // SL
    KT = N // P
    FIC = FI // P
    QS = 512
    NQS = T // QS
    NT = T // P
    TH = T // 2
    isl_bounds = []
    o = 0
    while o < FI:
        w = min(512, FI - o)
        isl_bounds.append((o, w))
        o += w

    nc = bass.Bass("TRN2", target_bir_lowering=False, debug=False)

    x_in = nc.dram_tensor("x", [D, N], F32, kind="ExternalInput").ap()
    wqkv_in = nc.dram_tensor("wqkv", [D, 3 * D], F32, kind="ExternalInput").ap()
    wproj_in = nc.dram_tensor("wproj", [D, D], F32, kind="ExternalInput").ap()
    bproj_in = nc.dram_tensor("bproj", [P, KC], F32, kind="ExternalInput").ap()
    cvec_in = nc.dram_tensor("cvec", [P, KC], F32, kind="ExternalInput").ap()
    wmod_in = nc.dram_tensor("wmod", [D, 6 * D], F32, kind="ExternalInput").ap()
    wgate_in = nc.dram_tensor("wgate", [FI, D], F32, kind="ExternalInput").ap()
    wup_in = nc.dram_tensor("wup", [FI, D], F32, kind="ExternalInput").ap()
    wdown_in = nc.dram_tensor("wdown", [D, FI], F32, kind="ExternalInput").ap()
    ident_in = nc.dram_tensor("ident", [P, P], F32, kind="ExternalInput").ap()
    out_dram = nc.dram_tensor("out", [D, T], F32, kind="ExternalOutput").ap()

    qscr = nc.dram_tensor("qscr", [D, T], BF16, kind="Internal").ap()
    kscr = nc.dram_tensor("kscr", [D, N], BF16, kind="Internal").ap()
    hscr = nc.dram_tensor("hscr", [D, N], BF16, kind="Internal").ap()
    wgts = nc.dram_tensor("wgts", [D, FI], BF16, kind="Internal").ap()
    wuts = nc.dram_tensor("wuts", [D, FI], BF16, kind="Internal").ap()
    wdts = nc.dram_tensor("wdts", [FI, D], BF16, kind="Internal").ap()

    xr = x_in.rearrange("(c p) n -> p c n", p=P)
    qscr_r = qscr.rearrange("(c p) n -> p c n", p=P)
    kscr_r = kscr.rearrange("(c p) n -> p c n", p=P)
    hscr_r = hscr.rearrange("(c p) n -> p c n", p=P)
    wgts_r = wgts.rearrange("(c p) f -> p c f", p=P)
    wuts_r = wuts.rearrange("(c p) f -> p c f", p=P)
    out_r = out_dram.rearrange("(c p) n -> p c n", p=P)

    es = contextlib.ExitStack()

    def sb(name, shape, dt, stack=None):
        return (stack or es).enter_context(nc.sbuf_tensor(name, shape, dt))

    def psum(name, parts, free, dt=F32, stack=None):
        esz = 4 if dt == F32 else 2
        banks = -(-free * esz // 2048)
        return (stack or es).enter_context(
            nc.psum_tensor(name, [parts, banks * 2048 // esz], dt))

    class Sem:
        def __init__(self, name):
            self.h = es.enter_context(nc.semaphore(name))
            self.v = 0

    NQSEM = 4
    QSEMS = {q: [Sem(f"d_{q}{i}") for i in range(NQSEM)] for q in ("sync", "gpsimd", "scalar")}
    QIDX = {q: 0 for q in QSEMS}
    mP, mV, mA = Sem("mP"), Sem("mV"), Sem("mA")
    ALL_SEMS = [mP, mV, mA] + [s for v in QSEMS.values() for s in v]

    x_own = sb("x_own", [P, KC, T], F32)
    ones128 = sb("ones128", [P, 1], BF16)
    ones1x = sb("ones1x", [1, P], F32)
    ident32 = sb("ident32", [P, P], F32)
    identbf = sb("identbf", [P, P], BF16)
    modc = sb("modc", [P, 6 * KC], F32)
    ca1 = sb("ca1", [P, KC], F32)
    gacol = sb("gacol", [P, KC], F32)
    cb1 = sb("cb1", [P, KC], F32)
    gbcol = sb("gbcol", [P, KC], F32)
    bpcol = sb("bpcol", [P, KC], F32)
    ccol = sb("ccol", [P, KC], F32)
    negc = sb("negc", [P, 1], F32)
    eps5 = sb("eps5", [P, 1], F32)
    eps30 = sb("eps30", [P, 1], F32)
    smallf = sb("smallf", [P, 64], F32)
    sa = modc[:, 0 * KC:1 * KC]
    sbm = modc[:, 3 * KC:4 * KC]

    with nc.Block("mk") as blk:
        E = {"sync": blk.sync, "gpsimd": blk.gpsimd, "scalar": blk.scalar,
             "vector": blk.vector, "tensor": blk.tensor}

        def em(eng, fn, inc=None, k=1):
            if inc is None:
                E[eng](fn)
                return None
            inc.v += k
            E[eng](lambda e, fn=fn, h=inc.h, k=k: fn(e).then_inc(h, k))
            return inc.v

        def _q_issue(q, fn):
            s = QSEMS[q][QIDX[q]]
            QIDX[q] = (QIDX[q] + 1) % NQSEM
            wt(q, s, s.v)
            em(q, fn, s, 16)
            return (s, s.v)

        def dma(q, dst, src):
            return _q_issue(q, lambda e, dst=dst, src=src: e.dma_start(dst, src))

        def dmaT(q, dst, src):
            return _q_issue(q, lambda e, dst=dst, src=src: e.dma_start_transpose(dst, src))

        def wtt(eng, tok):
            if tok:
                wt(eng, tok[0], tok[1])

        def qwait(eng, q):
            for s in QSEMS[q]:
                wt(eng, s, s.v)

        def wt(eng, sem, v=None):
            v = sem.v if v is None else v
            E[eng](lambda e, h=sem.h, v=v: e.wait_ge(h, v))

        def dr(eng):
            E[eng](lambda e: e.drain())

        def barrier():
            for eng in ("sync", "gpsimd", "scalar", "vector", "tensor"):
                for s in ALL_SEMS:
                    wt(eng, s)

        def act_table(set_id):
            E["scalar"](lambda e, i=set_id: e.add_instruction(
                mybir.InstLoadActFuncSet(
                    name=nc.get_next_instruction_name(),
                    act_func_set_id=i, ins=[], outs=[])))

        LAD = {}

        _lad_n = [0]

        def make_ladder(stack, QF):
            LAD.clear()
            _lad_n[0] += 1
            sfx = str(_lad_n[0])
            LAD["QF"] = QF
            LAD["bm"] = sb("lad_bm" + sfx, [P, QF // 16], F32, stack)
            LAD["s6"] = sb("lad_s6" + sfx, [P, QF // 16], F32, stack)
            LAD["rc"] = sb("lad_rc" + sfx, [P, QF // 16], F32, stack)
            LAD["ab"] = sb("lad_ab" + sfx, [P, QF], F32, stack)
            for nm in ("w1", "w2", "w3", "w4", "g2"):
                LAD[nm] = sb("lad_" + nm + sfx, [P, QF], BF16, stack)

        def quant_stages(x_ap, out_ap, F, lo=0):
            L = LAD
            NB = F // 16
            b0 = lo // 16
            x3 = x_ap.rearrange("p (b k) -> p b k", k=16)
            o3 = out_ap.rearrange("p (b k) -> p b k", k=16)
            ab3 = L["ab"][:, lo:lo + F].rearrange("p (b k) -> p b k", k=16)
            bm, s6, rc = (L[n][:, b0:b0 + NB] for n in ("bm", "s6", "rc"))
            ab, w1, w2, w3, w4, g2 = (L[n][:, lo:lo + F] for n in
                                      ("ab", "w1", "w2", "w3", "w4", "g2"))
            rcb = L["rc"][:, b0:b0 + NB, None].broadcast_to([P, NB, 16])
            s6b = L["s6"][:, b0:b0 + NB, None].broadcast_to([P, NB, 16])
            return [
                [lambda e: e.tensor_reduce(bm, x3, AX, ALU.max,
                                           apply_absolute_value=True)],
                [lambda e: e.reciprocal(s6, bm)],
                [lambda e: e.tensor_scalar(rc, s6, 6.0, 1e12, ALU.mult, ALU.min)],
                [lambda e: e.reciprocal(s6, rc),
                 lambda e: e.scalar_tensor_tensor(ab, x_ap, -1.0, x_ap,
                                                  ALU.mult, ALU.max)],
                [lambda e: e.tensor_tensor(ab3, ab3, rcb, ALU.mult)],
                [lambda e: e.tensor_scalar(w1, ab, 0.25, 0.5, ALU.is_gt, ALU.mult),
                 lambda e: e.tensor_scalar(w2, ab, 0.75, 0.5, ALU.is_gt, ALU.mult),
                 lambda e: e.tensor_scalar(w3, ab, 1.25, 0.5, ALU.is_gt, ALU.mult),
                 lambda e: e.tensor_scalar(w4, ab, 1.75, 0.5, ALU.is_gt, ALU.mult)],
                [lambda e: e.tensor_tensor(w1, w1, w2, ALU.add),
                 lambda e: e.tensor_tensor(w3, w3, w4, ALU.add),
                 lambda e: e.tensor_scalar(g2, ab, 2.5, 1.0, ALU.is_gt, ALU.mult)],
                [lambda e: e.tensor_tensor(w1, w1, w3, ALU.add),
                 lambda e: e.tensor_scalar(w2, ab, 3.5, 1.0, ALU.is_gt, ALU.mult),
                 lambda e: e.tensor_scalar(w4, ab, 5.0, 2.0, ALU.is_gt, ALU.mult)],
                [lambda e: e.tensor_tensor(g2, g2, w2, ALU.add)],
                [lambda e: e.tensor_tensor(w1, w1, g2, ALU.add)],
                [lambda e: e.tensor_tensor(w1, w1, w4, ALU.add)],
                [lambda e: e.scalar_tensor_tensor(g2, x_ap, 0.0, w1,
                                                  ALU.is_ge, ALU.mult)],
                [lambda e: e.scalar_tensor_tensor(w1, g2, 2.0, w1,
                                                  ALU.mult, ALU.subtract)],
                [lambda e: e.tensor_tensor(
                    o3, w1.rearrange("p (b k) -> p b k", k=16), s6b, ALU.mult)],
            ]

        def emit_quant(x_ap, out_ap, F, lo=0):
            V = blk.vector
            for stage in quant_stages(x_ap, out_ap, F, lo):
                for op in stage:
                    V(op)
                V(lambda e: e.drain())

        def emit_quant_pair(xa, oa, Fa, xb, ob, Fb):
            V = blk.vector
            for sa_, sb_ in zip(quant_stages(xa, oa, Fa, 0),
                                quant_stages(xb, ob, Fb, Fa)):
                for op in sa_:
                    V(op)
                for op in sb_:
                    V(op)
                V(lambda e: e.drain())

        trow2 = sb("trow2", [32, P], F32)
        silu_c = sb("silu_c", [P, KC], F32)
        with contextlib.ExitStack() as p0:
            mrow = sb("mrow0", [1, 2 * D], F32, p0)
            wmod_eb = [sb(f"wme{i}", [P, 2 * D], F32, p0) for i in range(2)]
            mrow_ps = psum("mrow_ps", 1, 2 * D, F32, p0)
            tp0_ps = psum("tp0_ps", P, 16, F32, p0)

            act_table(18)
            em("vector", lambda e: e.memset(ones128[:], 1.0))
            em("vector", lambda e: e.memset(ones1x[:], 1.0))
            em("vector", lambda e: e.memset(eps5[:], 1e-5), mV)
            em("vector", lambda e: e.memset(eps30[:], 1e-30), mV)
            t_id = dma("sync", ident32[:], ident_in[:])
            t_cc = dma("sync", ccol[:], cvec_in[:])
            t_bp = dma("sync", bpcol[:], bproj_in[:])
            for si, q in ((0, "sync"), (1, "scalar"), (2, "sync"), (3, "scalar")):
                dma(q, x_own[:, :, 512 * si:512 * (si + 1)],
                    xr[:, :, 512 * si:512 * (si + 1)])
            wtt("vector", t_id)
            em("vector", lambda e: e.tensor_copy(identbf[:], ident32[:]), mV)
            dr("vector")
            wtt("scalar", t_cc)
            wtt("scalar", t_bp)
            v_silu = em("scalar", lambda e: e.activation(silu_c[:], ccol[:], AF.Silu), mA)
            wm_done = [0, 0]
            wm_qs = ["gpsimd", "sync", "scalar"]
            for kc in range(KC):
                if kc >= 2:
                    wt(wm_qs[kc % 3], mP, wm_done[kc % 2])
                t_wm = dma(wm_qs[kc % 3], wmod_eb[kc % 2][:],
                           wmod_in[P * kc:P * (kc + 1), 0:2 * D])
                wtt("tensor", t_wm)
                if kc == 0:
                    wt("tensor", mA, v_silu)
                for nt in range(2 * D // 512):
                    wm_done[kc % 2] = em("tensor", lambda e, kc=kc, nt=nt: e.matmul(
                        mrow_ps[0:1, 512 * nt:512 * (nt + 1)],
                        silu_c[:, kc:kc + 1],
                        wmod_eb[kc % 2][:, 512 * nt:512 * (nt + 1)],
                        start=(kc == 0), stop=(kc == KC - 1)), mP)
            wt("scalar", mP, mP.v)
            a_mr = em("scalar", lambda e: e.copy(
                mrow[0:1, :], mrow_ps[0:1, 0:2 * D]), mA)
            wt("gpsimd", mA, a_mr)
            rtoks = [dma("gpsimd", trow2[m:m + 1, :], mrow[0:1, P * m:P * (m + 1)])
                     for m in range(2 * KC)]
            for tk in rtoks:
                wtt("tensor", tk)
            p_tp = em("tensor", lambda e: e.transpose(
                tp0_ps[:, 0:2 * KC], trow2[0:2 * KC, :], ident32[0:2 * KC, 0:2 * KC]), mP)
            wt("vector", mP, p_tp)
            em("vector", lambda e: e.tensor_copy(
                modc[:, 0:2 * KC], tp0_ps[:, 0:2 * KC]), mV)
            dr("vector")
            cam = modc[:, 1 * KC:2 * KC]
            em("vector", lambda e: e.tensor_scalar(ca1[:], cam, 1.0, None, ALU.add), mV)
            dr("vector")
            barrier()

        with contextlib.ExitStack() as p1:
            act_table(6)
            qkn = sb("qkn", [1, 256], F32, p1)
            m_row = [sb(f"m_row{i}", [1, SL], F32, p1) for i in range(2)]
            v_row = [sb(f"v_row{i}", [1, SL], F32, p1) for i in range(2)]
            rs_row = [sb(f"rs_row{i}", [1, SL], F32, p1) for i in range(2)]
            nmr_row = [sb(f"nmr_row{i}", [1, SL], F32, p1) for i in range(2)]
            wqk_sb = [sb(f"wqk{i}", [P, 2 * D], BF16, p1) for i in range(KC)]
            xo_buf = [sb(f"xo{i}", [P, KC, SL], F32, p1) for i in range(2)]
            xbch = [sb(f"xbch{i}", [P, SL], BF16, p1) for i in range(2)]
            sqch = [sb(f"sqch{i}", [P, SL], BF16, p1) for i in range(2)]
            h_buf = [sb(f"h{i}", [P, KC, SL], BF16, p1) for i in range(2)]
            qkst = [sb(f"qkst{i}", [P, SL], BF16, p1) for i in range(4)]
            sqb = [sb(f"sqb{i}", [P, SL], BF16, p1) for i in range(4)]
            hscr_sc = sb("hscr_sc", [P, KC, SL], F32, p1)

            qk_ps = [psum(f"qk_ps{i}", P, SL, F32, p1) for i in range(2)]
            st_ps = [psum(f"st_ps{i}", 1, SL, F32, p1) for i in range(2)]
            bc_ps = psum("bc_ps", P, SL, F32, p1)
            bc2_ps = psum("bc2_ps", P, SL, F32, p1)
            nrm_ps = [psum(f"nrm_ps{i}", 1, SL, F32, p1) for i in range(2)]

            em("vector", lambda e: e.memset(qkn[:], 0.0), mV)
            wqk_toks = [dma("gpsimd", wqk_sb[kc][:],
                            wqkv_in[P * kc:P * (kc + 1), 0:2 * D])
                        for kc in range(KC)]
            for tok in wqk_toks:
                wtt("tensor", tok)

            pe_done = [0] * NSL
            hdma_done = [None] * NSL
            ev_at = [0, 0]
            dma_at = [None, None]
            nrm_at = [0, 0]
            stat_done = [0, 0]
            bc_at = [0] * (NSL + 1)
            rowchain_nmr = [0] * (NSL + 1)

            def x_slice(s):
                return x_own[:, :, SL * s:SL * (s + 1)] if s < OSL else xo_buf[s % 2][:]

            def emit_load_cast_stats(s):
                bi = s % 2
                if s >= OSL:
                    if s >= OSL + 2:
                        wt("sync", mP, pe_done[s - 2])
                    t_xo = dma("sync", xo_buf[bi][:], xr[:, :, SL * s:SL * (s + 1)])
                    wtt("vector", t_xo)
                x_sl = x_slice(s)
                for kc in range(KC):
                    cb_ = kc % 2
                    if stat_done[cb_]:
                        wt("vector", mP, stat_done[cb_])
                    em("vector", lambda e, kc=kc, cb_=cb_, x_sl=x_sl: e.tensor_copy(
                        xbch[cb_][:], x_sl[:, kc, :]))
                    dr("vector")
                    v_c = em("vector", lambda e, kc=kc, cb_=cb_: e.tensor_tensor(
                        sqch[cb_][:], xbch[cb_][:], xbch[cb_][:], ALU.mult), mV)
                    wt("tensor", mV, v_c)
                    em("tensor", lambda e, kc=kc, cb_=cb_: e.matmul(
                        st_ps[0][:, 0:SL], ones128[:], xbch[cb_][:],
                        start=(kc == 0), stop=(kc == KC - 1)))
                    stat_done[cb_] = em("tensor", lambda e, kc=kc, cb_=cb_: e.matmul(
                        st_ps[1][:, 0:SL], ones128[:], sqch[cb_][:],
                        start=(kc == 0), stop=(kc == KC - 1)), mP)
                wt("vector", mP, stat_done[(KC - 1) % 2])
                em("vector", lambda e, bi=bi: e.tensor_scalar(
                    m_row[bi][:], st_ps[0][0:1, 0:SL], 1.0 / D, None, ALU.mult))
                em("vector", lambda e, bi=bi: e.tensor_scalar(
                    v_row[bi][:], st_ps[1][0:1, 0:SL], 1.0 / D, None, ALU.mult))
                dr("vector")
                em("vector", lambda e, bi=bi: e.tensor_tensor(
                    rs_row[bi][:], m_row[bi][:], m_row[bi][:], ALU.mult))
                dr("vector")
                v_var = em("vector", lambda e, bi=bi: e.tensor_tensor(
                    v_row[bi][:], v_row[bi][:], rs_row[bi][:], ALU.subtract), mV)
                wt("scalar", mV, v_var)
                em("scalar", lambda e, bi=bi: e.activation(
                    rs_row[bi][:], v_row[bi][:], AF.Ln, bias=eps5[0:1, :], scale=1.0))
                dr("scalar")
                a_rstd = em("scalar", lambda e, bi=bi: e.activation(
                    rs_row[bi][:], rs_row[bi][:], AF.Exp, scale=-0.5), mA)
                wt("vector", mA, a_rstd)
                rowchain_nmr[s] = em("vector", lambda e, bi=bi: e.scalar_tensor_tensor(
                    nmr_row[bi][:], m_row[bi][:], -1.0, rs_row[bi][:],
                    ALU.mult, ALU.mult), mV)
                dr("vector")

            def emit_bcast(s, v_h_prev):
                bi = s % 2
                wt("tensor", mV, rowchain_nmr[s])
                wt("tensor", mV, v_h_prev)
                em("tensor", lambda e, bi=bi: e.matmul(
                    bc_ps[:, 0:SL], ones1x[:], rs_row[bi][:], start=True, stop=True))
                bc_at[s] = em("tensor", lambda e, bi=bi: e.matmul(
                    bc2_ps[:, 0:SL], ones1x[:], nmr_row[bi][:], start=True, stop=True), mP)

            def emit_h(s):
                bi = s % 2
                wt("vector", mP, bc_at[s])
                v_h = 0
                for kc in range(KC):
                    em("vector", lambda e, kc=kc, s=s: e.tensor_tensor(
                        hscr_sc[:, kc, :], x_slice(s)[:, kc, :], bc_ps[:, 0:SL],
                        ALU.mult))
                dr("vector")
                for kc in range(KC):
                    em("vector", lambda e, kc=kc: e.tensor_tensor(
                        hscr_sc[:, kc, :], hscr_sc[:, kc, :], bc2_ps[:, 0:SL],
                        ALU.add))
                dr("vector")
                for kc in range(KC):
                    v_h = em("vector", lambda e, kc=kc, bi=bi: e.tensor_scalar(
                        h_buf[bi][:, kc, :], hscr_sc[:, kc, :],
                        ca1[:, kc:kc + 1], sa[:, kc:kc + 1], ALU.mult, ALU.add), mV)
                dr("vector")
                wt("gpsimd", mV, v_h)
                hdma_done[s] = dma("gpsimd", hscr_r[:, :, SL * s:SL * (s + 1)], h_buf[bi][:])
                return v_h

            NDEF = 2
            norm_q = []
            pop_mp = [0, 0, 0, 0]
            last_nrm_red = [0]

            def emit_norm_pop():
                if not norm_q:
                    return
                qb4, base_i, v_sq = norm_q.pop(0)
                wt("tensor", mV, v_sq)
                if last_nrm_red[0]:
                    wt("tensor", mV, last_nrm_red[0])
                for half in range(2):
                    em("tensor", lambda e, half=half, qb4=qb4: e.matmul(
                        nrm_ps[half][:, 0:SL],
                        ones128[half * 64:half * 64 + 64, :],
                        sqb[qb4][half * 64:half * 64 + 64, :],
                        start=True, stop=True),
                        mP if half == 1 else None)
                p_nrm = mP.v
                pop_mp[qb4] = p_nrm
                wt("vector", mP, p_nrm)
                for half in range(2):
                    em("vector", lambda e, half=half, base_i=base_i: e.tensor_reduce(
                        qkn[0:1, base_i + half:base_i + half + 1],
                        nrm_ps[half][0:1, 0:SL], AX, ALU.max),
                        mV if half == 1 else None)
                dr("vector")
                last_nrm_red[0] = mV.v
                return mV.v

            def emit_qk(s, v_h, j_range):
                bi = s % 2
                own = s < OSL
                oc_list = (list(range(0, KC)) if own else []) + list(range(KC, 2 * KC))
                for j in j_range:
                    if j >= len(oc_list):
                        return
                    oc = oc_list[j]
                    pb = j % 2
                    qb4 = emit_qk.ctr % 4
                    emit_qk.ctr += 1
                    if j == 0:
                        wt("tensor", mV, v_h)
                    if ev_at[pb]:
                        wt("tensor", mA, ev_at[pb])
                    for kc in range(KC):
                        em("tensor", lambda e, oc=oc, kc=kc, pb=pb, bi=bi: e.matmul(
                            qk_ps[pb][:, 0:SL], wqk_sb[kc][:, P * oc:P * (oc + 1)],
                            h_buf[bi][:, kc, :],
                            start=(kc == 0), stop=(kc == KC - 1)),
                            mP if kc == KC - 1 else None)
                    p_mm = mP.v
                    if len(norm_q) >= NDEF:
                        nrm_at[pb] = emit_norm_pop()
                    wt("scalar", mP, p_mm)
                    wtt("scalar", dma_at[pb])
                    if nrm_at[pb]:
                        wt("scalar", mV, nrm_at[pb])
                    ev_at[pb] = em("scalar", lambda e, pb=pb, qb4=qb4: e.copy(
                        qkst[qb4][:], qk_ps[pb][:, 0:SL]), mA)
                    wt("gpsimd", mA, ev_at[pb])
                    if oc < KC:
                        dma_at[pb] = dma("gpsimd", qscr_r[:, oc, SL * s:SL * (s + 1)], qkst[qb4][:])
                    else:
                        dma_at[pb] = dma("gpsimd", kscr_r[:, oc - KC, SL * s:SL * (s + 1)], qkst[qb4][:])
                    wt("vector", mA, ev_at[pb])
                    if pop_mp[qb4]:
                        wt("vector", mP, pop_mp[qb4])
                    v_sq = em("vector", lambda e, qb4=qb4: e.tensor_tensor(
                        sqb[qb4][:], qkst[qb4][:], qkst[qb4][:], ALU.mult), mV)
                    if oc < KC:
                        base_i = 2 * (s * KC + oc)
                    else:
                        base_i = 64 + 2 * (s * KC + (oc - KC))
                    norm_q.append((qb4, base_i, v_sq))
            emit_qk.ctr = 0

            emit_load_cast_stats(0)
            emit_bcast(0, 0)
            v_h_cur = emit_h(0)
            for s in range(NSL):
                own = s < OSL
                n_oc = 2 * KC if own else KC
                emit_qk(s, v_h_cur, range(0, 4))
                if s + 1 < NSL:
                    emit_load_cast_stats(s + 1)
                    emit_bcast(s + 1, v_h_cur)
                    v_h_next = emit_h(s + 1)
                else:
                    v_h_next = v_h_cur
                emit_qk(s, v_h_cur, range(4, n_oc))
                v_h_cur = v_h_next
                pe_done[s] = mP.v
            while norm_q:
                emit_norm_pop()
            barrier()

            em("vector", lambda e: e.tensor_reduce(
                smallf[0:1, 0:1], qkn[0:1, 0:64], AX, ALU.max))
            dr("vector")
            em("vector", lambda e: e.tensor_reduce(
                smallf[0:1, 1:2], qkn[0:1, 64:192], AX, ALU.max), mV)
            dr("vector")
            wt("scalar", mV, mV.v)
            em("scalar", lambda e: e.activation(
                smallf[0:1, 2:3], smallf[0:1, 0:1], AF.Ln, bias=eps30[0:1, :], scale=1.0))
            dr("scalar")
            em("scalar", lambda e: e.activation(
                smallf[0:1, 3:4], smallf[0:1, 1:2], AF.Ln, bias=eps30[0:1, :], scale=1.0), mA)
            dr("scalar")
            wt("vector", mA, mA.v)
            em("vector", lambda e: e.tensor_tensor(
                smallf[0:1, 4:5], smallf[0:1, 2:3], smallf[0:1, 3:4], ALU.add), mV)
            dr("vector")
            wt("scalar", mV, mV.v)
            em("scalar", lambda e: e.activation(
                smallf[0:1, 5:6], smallf[0:1, 4:5], AF.Exp, scale=0.5), mA)
            dr("scalar")
            wt("vector", mA, mA.v)
            em("vector", lambda e: e.tensor_scalar(
                smallf[0:1, 6:7], smallf[0:1, 5:6], -0.125, None, ALU.mult), mV)
            dr("vector")
            wt("tensor", mV, mV.v)
            em("tensor", lambda e: e.matmul(
                bc_ps[:, 0:1], ones1x[:], smallf[0:1, 6:7], start=True, stop=True), mP)
            wt("vector", mP, mP.v)
            em("vector", lambda e: e.tensor_copy(negc[:], bc_ps[:, 0:1]), mV)
            dr("vector")
            barrier()

        p34 = contextlib.ExitStack()
        vres = sb("vres", [P, KT, NH * 65], BF16, p34)
        O = sb("O", [P, KC, T], BF16, p34)
        with contextlib.ExitStack() as pv:
            wv_sb = [sb(f"wv{i}", [P, D], BF16, pv) for i in range(KC)]
            wmod_lb = [sb(f"wml{i}", [P, D], F32, pv) for i in range(2)]
            mrow = sb("mrowL", [1, 2 * D], F32, pv)
            hld = [sb(f"hld{i}", [P, KC, P], BF16, pv) for i in range(2)]
            v_ps = [psum(f"v_ps{i}", P, D, F32, pv) for i in range(2)]
            mrow2_ps = psum("mrow2_ps", 1, 2 * D, F32, pv)

            em("vector", lambda e: e.memset(
                vres[:].rearrange("p t (h m) -> p t h m", m=65)[:, :, :, 64:65], 1.0), mV)
            wv_toks = [dma("gpsimd", wv_sb[kc][:],
                           wqkv_in[P * kc:P * (kc + 1), 2 * D:3 * D])
                       for kc in range(KC)]
            for tok in wv_toks:
                wtt("tensor", tok)
            v_wv = mV.v

            ml_steps = [(grp, part, kc) for grp in range(2)
                        for part in range(2) for kc in range(KC)]
            ml_done = [0, 0]
            ml_ev = [0]
            ml_rtoks = []

            def emit_mod_late(i):
                if i >= len(ml_steps):
                    return
                grp, part, kc = ml_steps[i]
                lb = i % 2
                q = ("scalar", "gpsimd")[i % 2]
                if ml_done[lb]:
                    wt(q, mP, ml_done[lb])
                if grp == 1 and part == 0 and kc == 0 and ml_ev[0]:
                    pass
                coff = 2 * D + 2048 * grp + D * part
                t_l = dma(q, wmod_lb[lb][:], wmod_in[P * kc:P * (kc + 1),
                                                     coff:coff + D])
                wtt("tensor", t_l)
                if grp == 1 and part == 0 and kc == 0:
                    wt("tensor", mA, ml_ev[0])
                for nt in range(D // 512):
                    ml_done[lb] = em("tensor", lambda e, kc=kc, nt=nt, part=part: e.matmul(
                        mrow2_ps[0:1, D * part + 512 * nt:D * part + 512 * (nt + 1)],
                        silu_c[:, kc:kc + 1],
                        wmod_lb[lb][:, 512 * nt:512 * (nt + 1)],
                        start=(kc == 0), stop=(kc == KC - 1)), mP)
                if part == 1 and kc == KC - 1:
                    wt("scalar", mP, ml_done[lb])
                    if grp == 1:
                        for tk in ml_rtoks:
                            wtt("scalar", tk)
                    ml_ev[0] = em("scalar", lambda e: e.copy(
                        mrow[0:1, :], mrow2_ps[0:1, 0:2 * D]), mA)
                    wt("gpsimd", mA, ml_ev[0])
                    for m in range(2 * KC):
                        ml_rtoks.append(dma("gpsimd",
                                            trow2[16 * grp + m:16 * grp + m + 1, :],
                                            mrow[0:1, P * m:P * (m + 1)]))
            ev_v = [0, 0]
            mm_v = [0, 0]
            for kt in range(KT):
                tb = kt % 2
                if kt >= 2:
                    wt("sync", mP, mm_v[tb])
                t_h = dma("sync", hld[tb][:], hscr_r[:, :, P * kt:P * (kt + 1)])
                wtt("tensor", t_h)
                if kt == 0:
                    wt("tensor", mV, v_wv)
                if kt >= 2:
                    wt("tensor", mV, ev_v[tb])
                for half in range(2):
                    for kc in range(KC):
                        em("tensor", lambda e, kt=kt, kc=kc, half=half, tb=tb: e.matmul(
                            v_ps[tb][:, 512 * half:512 * (half + 1)],
                            hld[tb][:, kc, :],
                            wv_sb[kc][:, 512 * half:512 * (half + 1)],
                            start=(kc == 0), stop=(kc == KC - 1)),
                            mP if (kc == KC - 1 and half == 1) else None)
                mm_v[tb] = mP.v
                emit_mod_late(kt)
                wt("vector", mP, mm_v[tb])
                ev_v[tb] = em("vector", lambda e, kt=kt, tb=tb: e.tensor_copy(
                    vres[:, kt, :].rearrange("p (h m) -> p h m", m=65)[:, :, 0:64],
                    v_ps[tb][:, 0:D].rearrange("p (h m) -> p h m", m=64)), mV)
                dr("vector")
            for tk in ml_rtoks:
                wtt("tensor", tk)
            wt("tensor", mV, ev_v[0])
            p_tp2 = em("tensor", lambda e: e.transpose(
                v_ps[0][:, 0:32], trow2[0:32, :], ident32[0:32, 0:32]), mP)
            wt("vector", mP, p_tp2)
            em("vector", lambda e: e.tensor_copy(
                modc[:, 2 * KC:6 * KC], v_ps[0][:, 0:32]), mV)
            dr("vector")
            gam = modc[:, 2 * KC:3 * KC]
            cbm = modc[:, 4 * KC:5 * KC]
            gbm = modc[:, 5 * KC:6 * KC]
            em("vector", lambda e: e.tensor_scalar(cb1[:], cbm, 1.0, None, ALU.add))
            dr("vector")
            for g_src, g_dst in ((gam, gacol), (gbm, gbcol)):
                em("vector", lambda e, s=g_src, d=g_dst: e.scalar_tensor_tensor(
                    d[:], s, -1.0, s, ALU.mult, ALU.max))
                em("vector", lambda e, s=g_src: e.tensor_scalar(
                    smallf[:, 0:KC], s, 0.0, 2.0, ALU.is_ge, ALU.mult))
                dr("vector")
                em("vector", lambda e: e.tensor_scalar(
                    smallf[:, 0:KC], smallf[:, 0:KC], -1.0, None, ALU.add))
                em("vector", lambda e, d=g_dst: e.tensor_scalar(
                    d[:], d[:], 0.25, None, ALU.max))
                dr("vector")
                em("vector", lambda e, d=g_dst: e.tensor_tensor(
                    d[:], d[:], smallf[:, 0:KC], ALU.mult), mV)
                dr("vector")
            barrier()

        xa_f = x_own[:].rearrange("p a b -> p (a b)")
        LAD.clear()
        _lad_n[0] += 1
        LAD["QF"] = 1408
        wtile = [xa_f[:, 0:1408], xa_f[:, 1408:2816]]
        LAD["ab"] = xa_f[:, 2816:4224]
        LAD["bm"] = xa_f[:, 4224:4312]
        LAD["s6"] = xa_f[:, 4312:4400]
        LAD["rc"] = xa_f[:, 4400:4488]
        xa_b = xa_f[:, 4488:16384].bitcast(BF16)
        LAD["w1"] = xa_b[:, 0:1408]
        LAD["w2"] = xa_b[:, 1408:2816]
        LAD["w3"] = xa_b[:, 2816:4224]
        LAD["w4"] = xa_b[:, 4224:5632]
        LAD["g2"] = xa_b[:, 5632:7040]
        wqt = [xa_b[:, 7040:8448], xa_b[:, 8448:9856]]
        wstage = [xa_b[:, 9856:11264], xa_b[:, 11264:12672]]
        wdts_rr = wdts.rearrange("(c p) d -> p c d", p=P)
        pw_plan = ([(wgate_in, 0, D, wgts_r, 0, t) for t in range(FI // P)]
                   + [(wup_in, 0, D, wuts_r, 0, t) for t in range(FI // P)]
                   + [(wdown_in, 1408 * hf, 1408, wdts_rr, 11 * hf, t)
                      for t in range(KC) for hf in range(2)])
        pw_i = [0]
        pw_q = [0, 0]
        pw_tT = [None, None]
        pw_st = [None, None]
        pw_pending = []

        def flush_pw_store():
            if not pw_pending:
                return
            tb, dst_r, cbase, F, t, tok = pw_pending.pop(0)
            wtt("gpsimd", tok)
            pw_st[tb] = dma("gpsimd",
                dst_r[:, cbase:cbase + F // P, P * t:P * (t + 1)],
                wstage[tb][:, 0:F].rearrange("p (c q) -> p c q", q=P))

        def emit_pw_tile():
            if pw_i[0] >= len(pw_plan):
                flush_pw_store()
                return
            src_ap, coff, F, dst_r, cbase, t = pw_plan[pw_i[0]]
            pw_i[0] += 1
            tb = pw_i[0] % 2
            if pw_q[tb]:
                wt("gpsimd", mV, pw_q[tb])
            t_l = dma("gpsimd", wtile[tb][:, 0:F],
                      src_ap[P * t:P * (t + 1), coff:coff + F])
            flush_pw_store()
            wtt("vector", t_l)
            if pw_tT[tb] is not None:
                wtt("vector", pw_tT[tb])
            for off in range(0, F, LAD["QF"]):
                w = min(LAD["QF"], F - off)
                emit_quant(wtile[tb][:, off:off + w], wqt[tb][:, off:off + w], w)
            pw_q[tb] = em("vector", lambda e: e.memset(smallf[0:1, 60:61], 0.0), mV)
            wt("sync", mV, pw_q[tb])
            wtt("sync", pw_st[tb])
            pw_tT[tb] = dmaT("sync",
                wstage[tb][:, 0:F].rearrange("p (c q) -> p c q", q=P),
                wqt[tb][:, 0:F].rearrange("p (c q) -> p c q", q=P))
            pw_pending.append((tb, dst_r, cbase, F, t, pw_tT[tb]))

        with contextlib.ExitStack() as p3:
            act_table(6)
            kbuf = [sb(f"kb{i}", [P, N], BF16, p3) for i in range(2)]
            qbuf = [sb(f"qb{i}", [P, T], BF16, p3) for i in range(2)]
            pbuf = [sb(f"pb{i}", [P, 1024], BF16, p3) for i in range(4)]
            lnsums = sb("lnsums", [P, T], F32, p3)
            NPP = NH // 2
            with contextlib.ExitStack() as p3i:
                s_ps = [psum(f"s_ps{i}", P, 1024, F32, p3i) for i in range(2)]
                pv_ps = [psum(f"pv_ps{i}", 65, 1024, F32, p3i) for i in range(2)]

                kv_done = [None, None]
                pe_kv_free = [0, 0]
                prev_exp = {}
                prev_pv = {}
                pv_free = 0
                last_ln = 0
                last_lndma = None
                NG = 2 * KT
                for pp in range(NPP):
                    pb = pp % 2
                    h0, h1 = 2 * pp, 2 * pp + 1
                    if pp == 0:
                        t1 = dma("gpsimd", kbuf[0][:], kscr[0:P, :])
                        t2 = dma("gpsimd", qbuf[0][:], qscr[0:P, :])
                        kv_done[0] = (t1, t2)
                    if pp + 1 < NPP:
                        nb = (pp + 1) % 2
                        if pp >= 1:
                            wt("gpsimd", mP, pe_kv_free[nb])
                        t1 = dma("gpsimd", kbuf[nb][:], kscr[P * (pp + 1):P * (pp + 2), :])
                        t2 = dma("gpsimd", qbuf[nb][:], qscr[P * (pp + 1):P * (pp + 2), :])
                        kv_done[nb] = (t1, t2)
                    wtt("tensor", kv_done[pb][0])
                    wtt("tensor", kv_done[pb][1])
                    for qh in range(2):
                        exp_at = {}
                        pv_at = {}
                        for g in range(NG):
                            kt, qs2 = divmod(g, 2)
                            qoff = 1024 * qh + 512 * qs2
                            sAB = s_ps[g % 2]
                            pAB = pbuf[g % 4]
                            if g >= 2:
                                wt("tensor", mA, exp_at[g - 2])
                            elif prev_exp:
                                wt("tensor", mA, prev_exp[NG - 2 + g])
                            em("tensor", lambda e, kt=kt, qoff=qoff, pb=pb, sAB=sAB: e.matmul(
                                sAB[:, 0:512], kbuf[pb][0:64, P * kt:P * (kt + 1)],
                                qbuf[pb][0:64, qoff:qoff + 512],
                                start=True, stop=True, tile_position=(0, 0)))
                            em("tensor", lambda e, kt=kt, qoff=qoff, pb=pb, sAB=sAB: e.matmul(
                                sAB[:, 512:1024], kbuf[pb][64:128, P * kt:P * (kt + 1)],
                                qbuf[pb][64:128, qoff:qoff + 512],
                                start=True, stop=True, tile_position=(64, 0)), mP)
                            sc_at = mP.v
                            if g >= 2:
                                gp = g - 2
                                ktp, qs2p = divmod(gp, 2)
                                if gp == 0 and not (pp == 0 and qh == 0):
                                    wt("tensor", mV, pv_free)
                                    wt("tensor", mA, last_ln)
                                em("tensor", lambda e, ktp=ktp, qs2p=qs2p, gp=gp, h0=h0: e.matmul(
                                    pv_ps[0][:, 512 * qs2p:512 * (qs2p + 1)],
                                    vres[:, ktp, 65 * h0:65 * (h0 + 1)],
                                    pbuf[gp % 4][:, 0:512],
                                    start=(ktp == 0), stop=(ktp == KT - 1)))
                                pv_at[gp] = em("tensor",
                                               lambda e, ktp=ktp, qs2p=qs2p, gp=gp, h1=h1: e.matmul(
                                                   pv_ps[1][:, 512 * qs2p:512 * (qs2p + 1)],
                                                   vres[:, ktp, 65 * h1:65 * (h1 + 1)],
                                                   pbuf[gp % 4][:, 512:1024],
                                                   start=(ktp == 0), stop=(ktp == KT - 1)),
                                               mP)
                            wt("scalar", mP, sc_at)
                            if g >= 3 and g % 2 == 0:
                                wt("scalar", mP, pv_at[g - 3])
                            elif g < 3 and prev_pv:
                                wt("scalar", mP, prev_pv[NG - 3 + g])
                            exp_at[g] = em("scalar", lambda e, sAB=sAB, pAB=pAB: e.activation(
                                pAB[:, 0:1024], sAB[:, 0:1024], AF.Exp,
                                bias=negc[:], scale=0.125), mA)
                        for gp in (NG - 2, NG - 1):
                            ktp, qs2p = divmod(gp, 2)
                            wt("tensor", mA, exp_at[gp])
                            em("tensor", lambda e, ktp=ktp, qs2p=qs2p, h0=h0: e.matmul(
                                pv_ps[0][:, 512 * qs2p:512 * (qs2p + 1)],
                                vres[:, ktp, 65 * h0:65 * (h0 + 1)],
                                pbuf[gp % 4][:, 0:512],
                                start=False, stop=True))
                            pv_at[gp] = em("tensor", lambda e, ktp=ktp, qs2p=qs2p, h1=h1: e.matmul(
                                pv_ps[1][:, 512 * qs2p:512 * (qs2p + 1)],
                                vres[:, ktp, 65 * h1:65 * (h1 + 1)],
                                pbuf[gp % 4][:, 512:1024],
                                start=False, stop=True), mP)
                        pv_tail = mP.v
                        pe_kv_free[pb] = pv_tail
                        wt("scalar", mP, pv_tail)
                        for hi, hh in ((0, h0), (1, h1)):
                            wtt("scalar", last_lndma)
                            em("scalar", lambda e, hi=hi: e.activation(
                                lnsums[64:65, 0:1024], pv_ps[hi][64:65, 0:1024], AF.Ln),
                                mA)
                            last_ln = mA.v
                            wt("gpsimd", mA, last_ln)
                            last_lndma = dma("gpsimd",
                                lnsums[hh:hh + 1, 1024 * qh:1024 * (qh + 1)],
                                lnsums[64:65, 0:1024])
                        wt("vector", mP, pv_tail)
                        em("vector", lambda e, h0=h0, qh=qh: e.tensor_copy(
                            O[0:64, h0 // 2, 1024 * qh:1024 * (qh + 1)],
                            pv_ps[0][0:64, 0:1024]), mV)
                        dr("vector")
                        pv_free = em("vector", lambda e, h1=h1, qh=qh: e.tensor_copy(
                            O[64:128, h1 // 2, 1024 * qh:1024 * (qh + 1)],
                            pv_ps[1][0:64, 0:1024]), mV)
                        dr("vector")
                        for _ in range(4):
                            emit_pw_tile()
                        prev_exp = exp_at
                        prev_pv = pv_at
                while pw_pending or pw_i[0] < len(pw_plan):
                    emit_pw_tile()
                for tb in (0, 1):
                    wt("sync", mV, pw_q[tb])
                    wtt("sync", pw_tT[tb])
                    wtt("sync", pw_st[tb])
                t_xr = dma("sync", x_own[:], xr[:, :, 0:T])
            with contextlib.ExitStack() as p35:
                bc_ps2 = [psum(f"bc35{i}", P, T, F32, p35) for i in range(2)]
                qwait("scalar", "gpsimd")
                a_rcp = em("scalar", lambda e: e.activation(
                    lnsums[32:48, :], lnsums[0:16, :], AF.Exp, scale=-1.0), mA)
                stg_free = 0
                tt_done = [0, 0]
                for h in range(NH):
                    b_ = 64 * (h % 2)
                    hb = h % 2
                    wt("gpsimd", mA, a_rcp)
                    if stg_free:
                        wt("gpsimd", mP, stg_free)
                    t_stg = dma("gpsimd", lnsums[0:1, :], lnsums[32 + h:33 + h, :])
                    wtt("tensor", t_stg)
                    if tt_done[hb]:
                        wt("tensor", mV, tt_done[hb])
                    for qs in range(NQS):
                        em("tensor", lambda e, qs=qs, b_=b_, hb=hb: e.matmul(
                            bc_ps2[hb][b_:b_ + 64, QS * qs:QS * (qs + 1)],
                            ones1x[:, 0:64],
                            lnsums[0:1, QS * qs:QS * (qs + 1)],
                            start=True, stop=True, tile_position=(0, b_)),
                            mP if qs == NQS - 1 else None)
                    stg_free = mP.v
                    wt("vector", mP, stg_free)
                    tt_done[hb] = em("vector", lambda e, b_=b_, h=h, hb=hb: e.tensor_tensor(
                        O[b_:b_ + 64, h // 2, 0:T],
                        O[b_:b_ + 64, h // 2, 0:T],
                        bc_ps2[hb][b_:b_ + 64, 0:T], ALU.mult), mV)
                    dr("vector")
            barrier()

        with contextlib.ExitStack() as p4:
            wpj_all = sb("wpj_all", [P, KC, D], BF16, p4)
            pj_ps = [psum(f"pj_ps{i}", P, QS, F32, p4) for i in range(2)]
            ptmp = sb("ptmp", [P, QS], F32, p4)
            wpj_toks = [dma("gpsimd", wpj_all[:, kc, :],
                            wproj_in[P * kc:P * (kc + 1), :])
                        for kc in range(KC)]
            for tok in wpj_toks:
                wtt("tensor", tok)
                dr("vector")
            wt("tensor", mV, mV.v)
            ev_pj = [0, 0]
            for oc in range(KC):
                for qs in range(NQS):
                    jj = oc * NQS + qs
                    pb = jj % 2
                    if jj >= 2:
                        wt("tensor", mV, ev_pj[pb])
                    for kc in range(KC):
                        em("tensor", lambda e, oc=oc, qs=qs, kc=kc, pb=pb: e.matmul(
                            pj_ps[pb][:, 0:QS], wpj_all[:, kc, P * oc:P * (oc + 1)],
                            O[:, kc, QS * qs:QS * (qs + 1)],
                            start=(kc == 0), stop=(kc == KC - 1)),
                            mP if kc == KC - 1 else None)
                    p_pj = mP.v
                    wt("vector", mP, p_pj)
                    em("vector", lambda e, oc=oc, pb=pb: e.tensor_scalar(
                        ptmp[:], pj_ps[pb][:, 0:QS],
                        bpcol[:, oc:oc + 1], gacol[:, oc:oc + 1], ALU.add, ALU.mult))
                    dr("vector")
                    ev_pj[pb] = em("vector", lambda e, oc=oc, qs=qs: e.tensor_tensor(
                        x_own[:, oc, QS * qs:QS * (qs + 1)],
                        x_own[:, oc, QS * qs:QS * (qs + 1)], ptmp[:], ALU.add), mV)
                    dr("vector")
            barrier()
        p34.close()

        h2q = sb("h2q", [P, KC, T], BF16)
        with contextlib.ExitStack() as p5:
            act_table(6)
            make_ladder(p5, 2048)
            x_tm = sb("x_tm", [P, NT, D], BF16, p5)
            xcb = sb("xcb", [P, T], BF16, p5)
            tpf_ps = psum("tpf5", P, 512, F32, p5)
            bcf_ps = psum("bcf5", P, 512, F32, p5)
            cbfull = sb("cbfull", [P, D], BF16, p5)
            sbfull = sb("sbfull", [P, D], BF16, p5)
            trow = sb("trow", [8, P], F32, p5)
            rrow = LAD["ab"]
            cs1 = sb("cs1", [P, NT], F32, p5)
            cs2 = sb("cs2", [P, NT], F32, p5)
            rstd2 = sb("rstd2", [P, NT], F32, p5)
            nmr2 = sb("nmr2", [P, NT], F32, p5)
            h2a = [sb(f"h2a{i}", [P, D], BF16, p5) for i in range(2)]
            h2qt = [sb(f"h2qt{i}", [P, D], BF16, p5) for i in range(2)]

            for vec, dst in ((cb1, cbfull), (sbm, sbfull)):
                wt("tensor", mV, mV.v)
                qwait("tensor", "gpsimd")
                p_tr = em("tensor", lambda e, vec=vec: e.transpose(
                    tpf_ps[0:KC, 0:P], vec[:], ident32[:]), mP)
                wt("vector", mP, p_tr)
                v_tr = em("vector", lambda e: e.tensor_copy(
                    trow[0:KC, :], tpf_ps[0:KC, 0:P]), mV)
                dr("vector")
                wt("gpsimd", mV, v_tr)
                rtoks = [dma("gpsimd", rrow[0:1, P * kc:P * (kc + 1)], trow[kc:kc + 1, :])
                         for kc in range(KC)]
                for tk in rtoks:
                    wtt("tensor", tk)
                for half in range(D // 512):
                    p_b = em("tensor", lambda e, half=half: e.matmul(
                        bcf_ps[:, 0:512], ones1x[:],
                        rrow[0:1, 512 * half:512 * (half + 1)],
                        start=True, stop=True), mP)
                    wt("vector", mP, p_b)
                    em("vector", lambda e, half=half, dst=dst: e.tensor_copy(
                        dst[:, 512 * half:512 * (half + 1)], bcf_ps[:, 0:512]), mV)
                    dr("vector")
                    if half == 0:
                        wt("tensor", mV, mV.v)
            v_mod_full = mV.v

            cdma = []
            for kc in range(KC):
                for tk in cdma:
                    wtt("vector", tk)
                v_c = em("vector", lambda e, kc=kc: e.tensor_copy(
                    xcb[:], x_own[:, kc, :]), mV)
                dr("vector")
                wt("sync", mV, v_c)
                cdma = [dmaT("sync", x_tm[:, :, P * kc:P * (kc + 1)],
                             xcb[:].rearrange("p (c q) -> p c q", q=P))]
            qwait("vector", "sync")
            qwait("vector", "scalar")
            em("vector", lambda e: e.tensor_reduce(
                cs1[:, 0:NT], x_tm[:, :, :], AX, ALU.add))
            for tp in range(NT // 2):
                for j in range(2):
                    em("vector", lambda e, j=j, tp=tp: e.tensor_tensor(
                        LAD["w1"][:, j * D:(j + 1) * D],
                        x_tm[:, 2 * tp + j, :], x_tm[:, 2 * tp + j, :], ALU.mult))
                dr("vector")
                em("vector", lambda e, tp=tp: e.tensor_reduce(
                    cs2[:, 2 * tp:2 * tp + 2],
                    LAD["w1"][:, 0:2 * D].rearrange("p (t d) -> p t d", d=D),
                    AX, ALU.add))
                dr("vector")
            em("vector", lambda e: e.tensor_scalar(
                cs1[:, 0:NT], cs1[:, 0:NT], 1.0 / D, None, ALU.mult))
            em("vector", lambda e: e.tensor_scalar(
                cs2[:, 0:NT], cs2[:, 0:NT], 1.0 / D, None, ALU.mult))
            dr("vector")
            em("vector", lambda e: e.tensor_tensor(
                rstd2[:, 0:NT], cs1[:, 0:NT], cs1[:, 0:NT], ALU.mult))
            dr("vector")
            v_v2 = em("vector", lambda e: e.tensor_tensor(
                cs2[:, 0:NT], cs2[:, 0:NT], rstd2[:, 0:NT], ALU.subtract), mV)
            wt("scalar", mV, v_v2)
            em("scalar", lambda e: e.activation(
                rstd2[:, 0:NT], cs2[:, 0:NT], AF.Ln, bias=eps5[:], scale=1.0))
            dr("scalar")
            a_r2 = em("scalar", lambda e: e.activation(
                rstd2[:, 0:NT], rstd2[:, 0:NT], AF.Exp, scale=-0.5), mA)
            wt("vector", mA, a_r2)
            em("vector", lambda e: e.scalar_tensor_tensor(
                nmr2[:, 0:NT], cs1[:, 0:NT], -1.0, rstd2[:, 0:NT],
                ALU.mult, ALU.mult), mV)
            dr("vector")
            tq_done = [[], []]
            for tp in range(NT // 2):
                tts = (2 * tp, 2 * tp + 1)
                for j, tt in enumerate(tts):
                    em("vector", lambda e, j=j, tt=tt: e.tensor_scalar(
                        h2a[j][:], x_tm[:, tt, :], rstd2[:, tt:tt + 1],
                        nmr2[:, tt:tt + 1], ALU.mult, ALU.add))
                dr("vector")
                for j in range(2):
                    em("vector", lambda e, j=j: e.tensor_tensor(
                        h2a[j][:], h2a[j][:], cbfull[:], ALU.mult))
                dr("vector")
                for j in range(2):
                    em("vector", lambda e, j=j: e.tensor_tensor(
                        h2a[j][:], h2a[j][:], sbfull[:], ALU.add))
                dr("vector")
                for tb in range(2):
                    for tk in tq_done[tb]:
                        wtt("vector", tk)
                emit_quant_pair(h2a[0][:], h2qt[0][:], D, h2a[1][:], h2qt[1][:], D)
                v_q = em("vector", lambda e: e.memset(smallf[0:1, 62:63], 0.0), mV)
                wt("sync", mV, v_q)
                for j, tt in enumerate(tts):
                    tq_done[j] = [dmaT("sync", h2q[:, :, P * tt:P * (tt + 1)],
                                       h2qt[j][:].rearrange("p (c q) -> p c q", q=P))]
            barrier()

        with contextlib.ExitStack() as p6:
            act_table(18)
            make_ladder(p6, 2048)
            hq = sb("hq", [P, FIC, TH], BF16, p6)
            wg_sb = sb("wg_sb", [P, KC, 512], BF16, p6)
            wu_sb = sb("wu_sb", [P, KC, 512], BF16, p6)
            silu_sb = [sb(f"silu_sb{i}", [P, 1024], BF16, p6) for i in range(2)]
            hmul = [sb(f"hmul{i}", [P, 1024], BF16, p6) for i in range(2)]
            wd_sb = [sb(f"wd{i}", [P, 512], BF16, p6) for i in range(4)]
            wdma = [0, 0, 0, 0]

            for th in range(2):
                tbase = TH * th
                with contextlib.ExitStack() as pga:
                    g_ps = [psum(f"g_ps{th}_{i}", P, 1024, F32, pga) for i in range(2)]
                    u_ps = [psum(f"u_ps{th}_{i}", P, 1024, F32, pga) for i in range(2)]
                    a_si = [0, 0]
                    v_mul = [0, 0]
                    tdma = [[], []]
                    pi = [0]
                    pend_q = []

                    def flush_q():
                        if not pend_q:
                            return
                        if len(pend_q) == 2:
                            (t0, Wa, _, _, _), (t1, Wb, _, _, _) = pend_q
                            emit_quant_pair(hmul[t0][:, 0:Wa], hmul[t0][:, 0:Wa], Wa,
                                            hmul[t1][:, 0:Wb], hmul[t1][:, 0:Wb], Wb)
                        else:
                            t0, Wa, _, _, _ = pend_q[0]
                            emit_quant(hmul[t0][:, 0:Wa], hmul[t0][:, 0:Wa], Wa)
                        v_hq = em("vector", lambda e: e.memset(
                            smallf[0:1, 61:62], 0.0), mV)
                        wt("sync", mV, v_hq)
                        for tb_, W2_, io_, iw_, qts_ in pend_q:
                            tdma[tb_] = [dmaT("sync",
                                              hq[:, io_ // P:io_ // P + iw_ // P,
                                                 P * qt:P * (qt + 1)],
                                              hmul[tb_][:, iw_ * j:iw_ * (j + 1)].rearrange(
                                                  "p (c q) -> p c q", q=P))
                                         for j, qt in enumerate(qts_)]
                        pend_q.clear()
                    for ii, (io, iw) in enumerate(isl_bounds):
                        wt("sync", mP, mP.v)
                        t_g = dma("sync", wg_sb[:, :, 0:iw], wgts_r[:, :, io:io + iw])
                        t_u = dma("sync", wu_sb[:, :, 0:iw], wuts_r[:, :, io:io + iw])
                        wtt("tensor", t_g)
                        wtt("tensor", t_u)
                        pair = iw == 512
                        qsteps = [(qt, qt + 1) for qt in range(0, TH // P, 2)] \
                            if pair else [(qt,) for qt in range(TH // P)]
                        for qts in qsteps:
                            tb = pi[0] % 2
                            pi[0] += 1
                            W2 = iw * len(qts)
                            if a_si[tb]:
                                wt("tensor", mA, a_si[tb])
                            if v_mul[tb]:
                                wt("tensor", mV, v_mul[tb])
                            for j, qt in enumerate(qts):
                                off = 512 * j
                                for kc in range(KC):
                                    em("tensor", lambda e, kc=kc, qt=qt, iw=iw, tb=tb, tbase=tbase, off=off: e.matmul(
                                        g_ps[tb][:, off:off + iw],
                                        h2q[:, kc, tbase + P * qt:tbase + P * (qt + 1)],
                                        wg_sb[:, kc, 0:iw],
                                        start=(kc == 0), stop=(kc == KC - 1)))
                            p_gu = 0
                            for j, qt in enumerate(qts):
                                off = 512 * j
                                last_j = j == len(qts) - 1
                                for kc in range(KC):
                                    p_gu = em("tensor", lambda e, kc=kc, qt=qt, iw=iw, tb=tb, tbase=tbase, off=off: e.matmul(
                                        u_ps[tb][:, off:off + iw],
                                        h2q[:, kc, tbase + P * qt:tbase + P * (qt + 1)],
                                        wu_sb[:, kc, 0:iw],
                                        start=(kc == 0), stop=(kc == KC - 1)),
                                        mP if (kc == KC - 1 and last_j) else None)
                            WS = 512 * (len(qts) - 1) + iw
                            wt("scalar", mP, p_gu)
                            a_si[tb] = em("scalar", lambda e, WS=WS, tb=tb: e.activation(
                                silu_sb[tb][:, 0:WS], g_ps[tb][:, 0:WS], AF.Silu), mA)
                            wt("vector", mA, a_si[tb])
                            for tk in tdma[tb]:
                                wtt("vector", tk)
                            v_mul[tb] = em("vector", lambda e, WS=WS, tb=tb: e.tensor_tensor(
                                hmul[tb][:, 0:WS], silu_sb[tb][:, 0:WS], u_ps[tb][:, 0:WS],
                                ALU.mult), mV)
                            dr("vector")
                            pend_q.append((tb, W2, io, iw, qts))
                            if len(pend_q) == 2:
                                flush_q()
                    flush_q()
                    wt("tensor", mA, mA.v)
                with contextlib.ExitStack() as pdw:
                    dp_ps = [psum(f"dp_ps{th}_{i}", P, 512, F32, pdw) for i in range(8)]
                    v_res = 0
                    qwait("tensor", "sync")
                    qwait("tensor", "scalar")
                    for opp in range(2):
                        for ic in range(FIC):
                            ib = ic % 4
                            if wdma[ib]:
                                wt("sync", mP, wdma[ib])
                            t_wd = dma("sync", wd_sb[ib][:],
                                       wdts[P * ic:P * (ic + 1),
                                            512 * opp:512 * (opp + 1)])
                            wtt("tensor", t_wd)
                            if ic == 0:
                                wt("tensor", mV, mV.v)
                            for j in range(4):
                                for qs in range(2):
                                    wdma[ib] = em("tensor",
                                                  lambda e, ic=ic, ib=ib, j=j, qs=qs: e.matmul(
                                                      dp_ps[2 * j + qs][:, 0:512],
                                                      wd_sb[ib][:, P * j:P * (j + 1)],
                                                      hq[:, ic, 512 * qs:512 * (qs + 1)],
                                                      start=(ic == 0), stop=(ic == FIC - 1)),
                                                  mP)
                        p_d = mP.v
                        wt("vector", mP, p_d)
                        for j in range(4):
                            oc = 4 * opp + j
                            for qs in range(2):
                                v_res = em("vector",
                                           lambda e, j=j, qs=qs, oc=oc, tbase=tbase: e.scalar_tensor_tensor(
                                               x_own[:, oc, tbase + 512 * qs:tbase + 512 * (qs + 1)],
                                               dp_ps[2 * j + qs][:, 0:512],
                                               gbcol[:, oc:oc + 1],
                                               x_own[:, oc, tbase + 512 * qs:tbase + 512 * (qs + 1)],
                                               ALU.mult, ALU.add), mV)
                                dr("vector")
                    wt("sync", mV, v_res)
                    for kc in range(KC):
                        dma("sync", out_r[:, kc, tbase:tbase + TH],
                            x_own[:, kc, tbase:tbase + TH])
            qwait("sync", "sync")
            barrier()
    es.close()
    nc.finalize()
    return nc


_NC_CACHE = {}


def _get_nc():
    if "nc" not in _NC_CACHE:
        _NC_CACHE["nc"] = build(CFG)
    return _NC_CACHE["nc"]


def kernel(L0_attn_x, L0_attn_w_qkv, L0_attn_w_proj, L0_attn_b_proj,
           L0_ffn_w_gate, L0_ffn_w_up, L0_ffn_w_down, L0_c, L0_wmod):
    x = np.asarray(L0_attn_x, np.float32)
    wqkv = np.ascontiguousarray(np.asarray(L0_attn_w_qkv, np.float32))
    wproj = np.ascontiguousarray(np.asarray(L0_attn_w_proj, np.float32))
    bproj = np.asarray(L0_attn_b_proj, np.float32)
    wgate = np.ascontiguousarray(np.asarray(L0_ffn_w_gate, np.float32))
    wup = np.ascontiguousarray(np.asarray(L0_ffn_w_up, np.float32))
    wdown = np.ascontiguousarray(np.asarray(L0_ffn_w_down, np.float32))
    cmat = np.asarray(L0_c, np.float32)
    wmod = np.ascontiguousarray(np.asarray(L0_wmod, np.float32))

    N, D = CFG["N"], CFG["D"]
    T = N // 2
    KC = D // P
    ident = np.eye(P, dtype=np.float32)
    bp_col = np.ascontiguousarray(bproj.reshape(KC, P).T)

    nc = _get_nc()
    in_maps = []
    for core in range(8):
        b_, half = core // 2, core % 2
        xT = x[b_].T
        xroll = np.concatenate([xT[:, half * T:(half + 1) * T],
                                xT[:, (1 - half) * T:(2 - half) * T]], axis=1)
        c_col = np.ascontiguousarray(cmat[b_].reshape(KC, P).T)
        in_maps.append({
            "x": np.ascontiguousarray(xroll),
            "wqkv": wqkv, "wproj": wproj, "bproj": bp_col,
            "cvec": c_col, "wmod": wmod,
            "wgate": wgate, "wup": wup, "wdown": wdown,
            "ident": ident,
        })
    res = run_bass_kernel_spmd(nc, in_maps, core_ids=list(range(8)))
    out = np.empty((4, N, D), np.float32)
    for core in range(8):
        b_, half = core // 2, core % 2
        o = np.asarray(res.results[core]["out"])
        out[b_, half * T:(half + 1) * T, :] = o.T
    return out


if __name__ == "__main__":
    nc = build(CFG)
    n = sum(1 for _ in nc.all_instructions())
    print("built OK; instructions:", n)


# version 41
# speedup vs baseline: 1.0536x; 1.0536x over previous
import contextlib
import numpy as np
import ml_dtypes
import concourse.bass as bass
import concourse.mybir as mybir
from concourse.bass_utils import run_bass_kernel_spmd

F32 = mybir.dt.float32
BF16 = mybir.dt.bfloat16
I16 = mybir.dt.int16
ALU = mybir.AluOpType
AF = mybir.ActivationFunctionType
AX = mybir.AxisListType.X

P = 128
CFG = dict(N=4096, D=1024, NH=16, FI=2816, SL=512)


def build(cfg):
    N, D, NH, FI, SL = cfg["N"], cfg["D"], cfg["NH"], cfg["FI"], cfg["SL"]
    T = N // 2
    KC = D // P
    NSL = N // SL
    OSL = T // SL
    KT = N // P
    FIC = FI // P
    QS = 512
    NQS = T // QS
    NT = T // P
    TH = T // 2
    isl_bounds = []
    o = 0
    while o < FI:
        w = min(512, FI - o)
        isl_bounds.append((o, w))
        o += w

    nc = bass.Bass("TRN2", target_bir_lowering=False, debug=False)

    x_in = nc.dram_tensor("x", [D, N], F32, kind="ExternalInput").ap()
    wqkv_in = nc.dram_tensor("wqkv", [D, 3 * D], F32, kind="ExternalInput").ap()
    wproj_in = nc.dram_tensor("wproj", [D, D], F32, kind="ExternalInput").ap()
    bproj_in = nc.dram_tensor("bproj", [P, KC], F32, kind="ExternalInput").ap()
    cvec_in = nc.dram_tensor("cvec", [P, KC], F32, kind="ExternalInput").ap()
    wmod_in = nc.dram_tensor("wmod", [D, 6 * D], F32, kind="ExternalInput").ap()
    wgate_in = nc.dram_tensor("wgate", [FI, D], F32, kind="ExternalInput").ap()
    wup_in = nc.dram_tensor("wup", [FI, D], F32, kind="ExternalInput").ap()
    wdown_in = nc.dram_tensor("wdown", [D, FI], F32, kind="ExternalInput").ap()
    ident_in = nc.dram_tensor("ident", [P, P], F32, kind="ExternalInput").ap()
    out_dram = nc.dram_tensor("out", [D, T], F32, kind="ExternalOutput").ap()

    qscr = nc.dram_tensor("qscr", [D, T], BF16, kind="Internal").ap()
    kscr = nc.dram_tensor("kscr", [D, N], BF16, kind="Internal").ap()
    hscr = nc.dram_tensor("hscr", [D, N], BF16, kind="Internal").ap()
    wgts = nc.dram_tensor("wgts", [D, FI], BF16, kind="Internal").ap()
    wuts = nc.dram_tensor("wuts", [D, FI], BF16, kind="Internal").ap()
    wdts = nc.dram_tensor("wdts", [FI, D], BF16, kind="Internal").ap()

    xr = x_in.rearrange("(c p) n -> p c n", p=P)
    qscr_r = qscr.rearrange("(c p) n -> p c n", p=P)
    kscr_r = kscr.rearrange("(c p) n -> p c n", p=P)
    hscr_r = hscr.rearrange("(c p) n -> p c n", p=P)
    wgts_r = wgts.rearrange("(c p) f -> p c f", p=P)
    wuts_r = wuts.rearrange("(c p) f -> p c f", p=P)
    out_r = out_dram.rearrange("(c p) n -> p c n", p=P)

    es = contextlib.ExitStack()

    def sb(name, shape, dt, stack=None):
        return (stack or es).enter_context(nc.sbuf_tensor(name, shape, dt))

    def psum(name, parts, free, dt=F32, stack=None):
        esz = 4 if dt == F32 else 2
        banks = -(-free * esz // 2048)
        return (stack or es).enter_context(
            nc.psum_tensor(name, [parts, banks * 2048 // esz], dt))

    class Sem:
        def __init__(self, name):
            self.h = es.enter_context(nc.semaphore(name))
            self.v = 0

    NQSEM = 4
    QSEMS = {q: [Sem(f"d_{q}{i}") for i in range(NQSEM)] for q in ("sync", "gpsimd", "scalar")}
    QIDX = {q: 0 for q in QSEMS}
    mP, mV, mA = Sem("mP"), Sem("mV"), Sem("mA")
    ALL_SEMS = [mP, mV, mA] + [s for v in QSEMS.values() for s in v]

    x_own = sb("x_own", [P, KC, T], F32)
    ones128 = sb("ones128", [P, 1], BF16)
    ones1x = sb("ones1x", [1, P], F32)
    ident32 = sb("ident32", [P, P], F32)
    identbf = sb("identbf", [P, P], BF16)
    modc = sb("modc", [P, 6 * KC], F32)
    ca1 = sb("ca1", [P, KC], F32)
    gacol = sb("gacol", [P, KC], F32)
    cb1 = sb("cb1", [P, KC], F32)
    gbcol = sb("gbcol", [P, KC], F32)
    bpcol = sb("bpcol", [P, KC], F32)
    ccol = sb("ccol", [P, KC], F32)
    negc = sb("negc", [P, 1], F32)
    eps5 = sb("eps5", [P, 1], F32)
    eps30 = sb("eps30", [P, 1], F32)
    smallf = sb("smallf", [P, 64], F32)
    sa = modc[:, 0 * KC:1 * KC]
    sbm = modc[:, 3 * KC:4 * KC]

    with nc.Block("mk") as blk:
        E = {"sync": blk.sync, "gpsimd": blk.gpsimd, "scalar": blk.scalar,
             "vector": blk.vector, "tensor": blk.tensor}

        def em(eng, fn, inc=None, k=1):
            if inc is None:
                E[eng](fn)
                return None
            inc.v += k
            E[eng](lambda e, fn=fn, h=inc.h, k=k: fn(e).then_inc(h, k))
            return inc.v

        def _q_issue(q, fn):
            s = QSEMS[q][QIDX[q]]
            QIDX[q] = (QIDX[q] + 1) % NQSEM
            wt(q, s, s.v)
            em(q, fn, s, 16)
            return (s, s.v)

        def dma(q, dst, src):
            return _q_issue(q, lambda e, dst=dst, src=src: e.dma_start(dst, src))

        def dmaT(q, dst, src):
            return _q_issue(q, lambda e, dst=dst, src=src: e.dma_start_transpose(dst, src))

        def wtt(eng, tok):
            if tok:
                wt(eng, tok[0], tok[1])

        def qwait(eng, q):
            for s in QSEMS[q]:
                wt(eng, s, s.v)

        def wt(eng, sem, v=None):
            v = sem.v if v is None else v
            E[eng](lambda e, h=sem.h, v=v: e.wait_ge(h, v))

        def dr(eng):
            E[eng](lambda e: e.drain())

        def barrier():
            for eng in ("sync", "gpsimd", "scalar", "vector", "tensor"):
                for s in ALL_SEMS:
                    wt(eng, s)

        def act_table(set_id):
            E["scalar"](lambda e, i=set_id: e.add_instruction(
                mybir.InstLoadActFuncSet(
                    name=nc.get_next_instruction_name(),
                    act_func_set_id=i, ins=[], outs=[])))

        LAD = {}

        _lad_n = [0]

        def make_ladder(stack, QF):
            LAD.clear()
            _lad_n[0] += 1
            sfx = str(_lad_n[0])
            LAD["QF"] = QF
            LAD["bm"] = sb("lad_bm" + sfx, [P, QF // 16], F32, stack)
            LAD["s6"] = sb("lad_s6" + sfx, [P, QF // 16], F32, stack)
            LAD["rc"] = sb("lad_rc" + sfx, [P, QF // 16], F32, stack)
            LAD["ab"] = sb("lad_ab" + sfx, [P, QF], F32, stack)
            for nm in ("w1", "w2", "w3", "w4", "g2"):
                LAD[nm] = sb("lad_" + nm + sfx, [P, QF], BF16, stack)

        def quant_stages(x_ap, out_ap, F, lo=0):
            L = LAD
            NB = F // 16
            b0 = lo // 16
            x3 = x_ap.rearrange("p (b k) -> p b k", k=16)
            o3 = out_ap.rearrange("p (b k) -> p b k", k=16)
            ab3 = L["ab"][:, lo:lo + F].rearrange("p (b k) -> p b k", k=16)
            bm, s6, rc = (L[n][:, b0:b0 + NB] for n in ("bm", "s6", "rc"))
            ab, w1, w2, w3, w4, g2 = (L[n][:, lo:lo + F] for n in
                                      ("ab", "w1", "w2", "w3", "w4", "g2"))
            rcb = L["rc"][:, b0:b0 + NB, None].broadcast_to([P, NB, 16])
            s6b = L["s6"][:, b0:b0 + NB, None].broadcast_to([P, NB, 16])
            return [
                [lambda e: e.tensor_reduce(bm, x3, AX, ALU.max,
                                           apply_absolute_value=True)],
                [lambda e: e.reciprocal(s6, bm)],
                [lambda e: e.tensor_scalar(rc, s6, 6.0, 1e12, ALU.mult, ALU.min)],
                [lambda e: e.reciprocal(s6, rc),
                 lambda e: e.scalar_tensor_tensor(ab, x_ap, -1.0, x_ap,
                                                  ALU.mult, ALU.max)],
                [lambda e: e.tensor_tensor(ab3, ab3, rcb, ALU.mult)],
                [lambda e: e.tensor_scalar(w1, ab, 0.25, 0.5, ALU.is_gt, ALU.mult),
                 lambda e: e.tensor_scalar(w2, ab, 0.75, 0.5, ALU.is_gt, ALU.mult),
                 lambda e: e.tensor_scalar(w3, ab, 1.25, 0.5, ALU.is_gt, ALU.mult),
                 lambda e: e.tensor_scalar(w4, ab, 1.75, 0.5, ALU.is_gt, ALU.mult)],
                [lambda e: e.tensor_tensor(w1, w1, w2, ALU.add),
                 lambda e: e.tensor_tensor(w3, w3, w4, ALU.add),
                 lambda e: e.tensor_scalar(g2, ab, 2.5, 1.0, ALU.is_gt, ALU.mult)],
                [lambda e: e.tensor_tensor(w1, w1, w3, ALU.add),
                 lambda e: e.tensor_scalar(w2, ab, 3.5, 1.0, ALU.is_gt, ALU.mult),
                 lambda e: e.tensor_scalar(w4, ab, 5.0, 2.0, ALU.is_gt, ALU.mult)],
                [lambda e: e.tensor_tensor(g2, g2, w2, ALU.add)],
                [lambda e: e.tensor_tensor(w1, w1, g2, ALU.add)],
                [lambda e: e.tensor_tensor(w1, w1, w4, ALU.add)],
                [lambda e: e.scalar_tensor_tensor(g2, x_ap, 0.0, w1,
                                                  ALU.is_ge, ALU.mult)],
                [lambda e: e.scalar_tensor_tensor(w1, g2, 2.0, w1,
                                                  ALU.mult, ALU.subtract)],
                [lambda e: e.tensor_tensor(
                    o3, w1.rearrange("p (b k) -> p b k", k=16), s6b, ALU.mult)],
            ]

        def emit_quant(x_ap, out_ap, F, lo=0):
            V = blk.vector
            for stage in quant_stages(x_ap, out_ap, F, lo):
                for op in stage:
                    V(op)
                V(lambda e: e.drain())

        def emit_quant_pair(xa, oa, Fa, xb, ob, Fb):
            V = blk.vector
            for sa_, sb_ in zip(quant_stages(xa, oa, Fa, 0),
                                quant_stages(xb, ob, Fb, Fa)):
                for op in sa_:
                    V(op)
                for op in sb_:
                    V(op)
                V(lambda e: e.drain())

        trow2 = sb("trow2", [32, P], F32)
        silu_c = sb("silu_c", [P, KC], F32)
        with contextlib.ExitStack() as p0:
            mrow = sb("mrow0", [1, 2 * D], F32, p0)
            wmod_eb = [sb(f"wme{i}", [P, 2 * D], F32, p0) for i in range(2)]
            mrow_ps = psum("mrow_ps", 1, 2 * D, F32, p0)
            tp0_ps = psum("tp0_ps", P, 16, F32, p0)

            act_table(18)
            em("vector", lambda e: e.memset(ones128[:], 1.0))
            em("vector", lambda e: e.memset(ones1x[:], 1.0))
            em("vector", lambda e: e.memset(eps5[:], 1e-5), mV)
            em("vector", lambda e: e.memset(eps30[:], 1e-30), mV)
            t_id = dma("sync", ident32[:], ident_in[:])
            t_cc = dma("sync", ccol[:], cvec_in[:])
            t_bp = dma("sync", bpcol[:], bproj_in[:])
            for si, q in ((0, "sync"), (1, "scalar"), (2, "sync"), (3, "scalar")):
                dma(q, x_own[:, :, 512 * si:512 * (si + 1)],
                    xr[:, :, 512 * si:512 * (si + 1)])
            wtt("vector", t_id)
            em("vector", lambda e: e.tensor_copy(identbf[:], ident32[:]), mV)
            dr("vector")
            wtt("scalar", t_cc)
            wtt("scalar", t_bp)
            v_silu = em("scalar", lambda e: e.activation(silu_c[:], ccol[:], AF.Silu), mA)
            wm_done = [0, 0]
            wm_qs = ["gpsimd", "sync", "scalar"]
            for kc in range(KC):
                if kc >= 2:
                    wt(wm_qs[kc % 3], mP, wm_done[kc % 2])
                t_wm = dma(wm_qs[kc % 3], wmod_eb[kc % 2][:],
                           wmod_in[P * kc:P * (kc + 1), 0:2 * D])
                wtt("tensor", t_wm)
                if kc == 0:
                    wt("tensor", mA, v_silu)
                for nt in range(2 * D // 512):
                    wm_done[kc % 2] = em("tensor", lambda e, kc=kc, nt=nt: e.matmul(
                        mrow_ps[0:1, 512 * nt:512 * (nt + 1)],
                        silu_c[:, kc:kc + 1],
                        wmod_eb[kc % 2][:, 512 * nt:512 * (nt + 1)],
                        start=(kc == 0), stop=(kc == KC - 1)), mP)
            wt("scalar", mP, mP.v)
            a_mr = em("scalar", lambda e: e.copy(
                mrow[0:1, :], mrow_ps[0:1, 0:2 * D]), mA)
            wt("gpsimd", mA, a_mr)
            rtoks = [dma("gpsimd", trow2[m:m + 1, :], mrow[0:1, P * m:P * (m + 1)])
                     for m in range(2 * KC)]
            for tk in rtoks:
                wtt("tensor", tk)
            p_tp = em("tensor", lambda e: e.transpose(
                tp0_ps[:, 0:2 * KC], trow2[0:2 * KC, :], ident32[0:2 * KC, 0:2 * KC]), mP)
            wt("vector", mP, p_tp)
            em("vector", lambda e: e.tensor_copy(
                modc[:, 0:2 * KC], tp0_ps[:, 0:2 * KC]), mV)
            dr("vector")
            cam = modc[:, 1 * KC:2 * KC]
            em("vector", lambda e: e.tensor_scalar(ca1[:], cam, 1.0, None, ALU.add), mV)
            dr("vector")
            barrier()

        with contextlib.ExitStack() as p1:
            act_table(6)
            qkn = sb("qkn", [1, 256], F32, p1)
            m_row = [sb(f"m_row{i}", [1, SL], F32, p1) for i in range(2)]
            v_row = [sb(f"v_row{i}", [1, SL], F32, p1) for i in range(2)]
            rs_row = [sb(f"rs_row{i}", [1, SL], F32, p1) for i in range(2)]
            nmr_row = [sb(f"nmr_row{i}", [1, SL], F32, p1) for i in range(2)]
            wqk_sb = [sb(f"wqk{i}", [P, 2 * D], BF16, p1) for i in range(KC)]
            xo_buf = [sb(f"xo{i}", [P, KC, SL], F32, p1) for i in range(2)]
            xbch = [sb(f"xbch{i}", [P, SL], BF16, p1) for i in range(2)]
            sqch = [sb(f"sqch{i}", [P, SL], BF16, p1) for i in range(2)]
            h_buf = [sb(f"h{i}", [P, KC, SL], BF16, p1) for i in range(2)]
            qkst = [sb(f"qkst{i}", [P, SL], BF16, p1) for i in range(4)]
            sqb = [sb(f"sqb{i}", [P, SL], BF16, p1) for i in range(4)]
            hscr_sc = sb("hscr_sc", [P, KC, SL], F32, p1)

            qk_ps = [psum(f"qk_ps{i}", P, SL, F32, p1) for i in range(2)]
            st_ps = [psum(f"st_ps{i}", 1, SL, F32, p1) for i in range(2)]
            bc_ps = psum("bc_ps", P, SL, F32, p1)
            bc2_ps = psum("bc2_ps", P, SL, F32, p1)
            nrm_ps = [psum(f"nrm_ps{i}", 1, SL, F32, p1) for i in range(2)]

            em("vector", lambda e: e.memset(qkn[:], 0.0), mV)
            wqk_toks = [dma("gpsimd", wqk_sb[kc][:],
                            wqkv_in[P * kc:P * (kc + 1), 0:2 * D])
                        for kc in range(KC)]
            for tok in wqk_toks:
                wtt("tensor", tok)

            pe_done = [0] * NSL
            hdma_done = [None] * NSL
            ev_at = [0, 0]
            dma_at = [None, None]
            nrm_at = [0, 0]
            stat_done = [0, 0]
            bc_at = [0] * (NSL + 1)
            rowchain_nmr = [0] * (NSL + 1)

            def x_slice(s):
                return x_own[:, :, SL * s:SL * (s + 1)] if s < OSL else xo_buf[s % 2][:]

            def emit_load_cast_stats(s):
                bi = s % 2
                if s >= OSL:
                    if s >= OSL + 2:
                        wt("sync", mP, pe_done[s - 2])
                    t_xo = dma("sync", xo_buf[bi][:], xr[:, :, SL * s:SL * (s + 1)])
                    wtt("vector", t_xo)
                x_sl = x_slice(s)
                for kc in range(KC):
                    cb_ = kc % 2
                    if stat_done[cb_]:
                        wt("vector", mP, stat_done[cb_])
                    em("vector", lambda e, kc=kc, cb_=cb_, x_sl=x_sl: e.tensor_copy(
                        xbch[cb_][:], x_sl[:, kc, :]))
                    dr("vector")
                    v_c = em("vector", lambda e, kc=kc, cb_=cb_: e.tensor_tensor(
                        sqch[cb_][:], xbch[cb_][:], xbch[cb_][:], ALU.mult), mV)
                    wt("tensor", mV, v_c)
                    em("tensor", lambda e, kc=kc, cb_=cb_: e.matmul(
                        st_ps[0][:, 0:SL], ones128[:], xbch[cb_][:],
                        start=(kc == 0), stop=(kc == KC - 1)))
                    stat_done[cb_] = em("tensor", lambda e, kc=kc, cb_=cb_: e.matmul(
                        st_ps[1][:, 0:SL], ones128[:], sqch[cb_][:],
                        start=(kc == 0), stop=(kc == KC - 1)), mP)
                wt("vector", mP, stat_done[(KC - 1) % 2])
                em("vector", lambda e, bi=bi: e.tensor_scalar(
                    m_row[bi][:], st_ps[0][0:1, 0:SL], 1.0 / D, None, ALU.mult))
                em("vector", lambda e, bi=bi: e.tensor_scalar(
                    v_row[bi][:], st_ps[1][0:1, 0:SL], 1.0 / D, None, ALU.mult))
                dr("vector")
                em("vector", lambda e, bi=bi: e.tensor_tensor(
                    rs_row[bi][:], m_row[bi][:], m_row[bi][:], ALU.mult))
                dr("vector")
                v_var = em("vector", lambda e, bi=bi: e.tensor_tensor(
                    v_row[bi][:], v_row[bi][:], rs_row[bi][:], ALU.subtract), mV)
                wt("scalar", mV, v_var)
                em("scalar", lambda e, bi=bi: e.activation(
                    rs_row[bi][:], v_row[bi][:], AF.Ln, bias=eps5[0:1, :], scale=1.0))
                dr("scalar")
                a_rstd = em("scalar", lambda e, bi=bi: e.activation(
                    rs_row[bi][:], rs_row[bi][:], AF.Exp, scale=-0.5), mA)
                wt("vector", mA, a_rstd)
                rowchain_nmr[s] = em("vector", lambda e, bi=bi: e.scalar_tensor_tensor(
                    nmr_row[bi][:], m_row[bi][:], -1.0, rs_row[bi][:],
                    ALU.mult, ALU.mult), mV)
                dr("vector")

            def emit_bcast(s, v_h_prev):
                bi = s % 2
                wt("tensor", mV, rowchain_nmr[s])
                wt("tensor", mV, v_h_prev)
                em("tensor", lambda e, bi=bi: e.matmul(
                    bc_ps[:, 0:SL], ones1x[:], rs_row[bi][:], start=True, stop=True))
                bc_at[s] = em("tensor", lambda e, bi=bi: e.matmul(
                    bc2_ps[:, 0:SL], ones1x[:], nmr_row[bi][:], start=True, stop=True), mP)

            def emit_h(s):
                bi = s % 2
                wt("vector", mP, bc_at[s])
                v_h = 0
                for kc in range(KC):
                    em("vector", lambda e, kc=kc, s=s: e.tensor_tensor(
                        hscr_sc[:, kc, :], x_slice(s)[:, kc, :], bc_ps[:, 0:SL],
                        ALU.mult))
                dr("vector")
                for kc in range(KC):
                    em("vector", lambda e, kc=kc: e.tensor_tensor(
                        hscr_sc[:, kc, :], hscr_sc[:, kc, :], bc2_ps[:, 0:SL],
                        ALU.add))
                dr("vector")
                for kc in range(KC):
                    v_h = em("vector", lambda e, kc=kc, bi=bi: e.tensor_scalar(
                        h_buf[bi][:, kc, :], hscr_sc[:, kc, :],
                        ca1[:, kc:kc + 1], sa[:, kc:kc + 1], ALU.mult, ALU.add), mV)
                dr("vector")
                wt("gpsimd", mV, v_h)
                hdma_done[s] = dma("gpsimd", hscr_r[:, :, SL * s:SL * (s + 1)], h_buf[bi][:])
                return v_h

            NDEF = 2
            norm_q = []
            pop_mp = [0, 0, 0, 0]
            last_nrm_red = [0]

            def emit_norm_pop():
                if not norm_q:
                    return
                qb4, base_i, v_sq = norm_q.pop(0)
                wt("tensor", mV, v_sq)
                if last_nrm_red[0]:
                    wt("tensor", mV, last_nrm_red[0])
                for half in range(2):
                    em("tensor", lambda e, half=half, qb4=qb4: e.matmul(
                        nrm_ps[half][:, 0:SL],
                        ones128[half * 64:half * 64 + 64, :],
                        sqb[qb4][half * 64:half * 64 + 64, :],
                        start=True, stop=True),
                        mP if half == 1 else None)
                p_nrm = mP.v
                pop_mp[qb4] = p_nrm
                wt("vector", mP, p_nrm)
                for half in range(2):
                    em("vector", lambda e, half=half, base_i=base_i: e.tensor_reduce(
                        qkn[0:1, base_i + half:base_i + half + 1],
                        nrm_ps[half][0:1, 0:SL], AX, ALU.max),
                        mV if half == 1 else None)
                dr("vector")
                last_nrm_red[0] = mV.v
                return mV.v

            def emit_qk(s, v_h, j_range):
                bi = s % 2
                own = s < OSL
                oc_list = (list(range(0, KC)) if own else []) + list(range(KC, 2 * KC))
                for j in j_range:
                    if j >= len(oc_list):
                        return
                    oc = oc_list[j]
                    pb = j % 2
                    qb4 = emit_qk.ctr % 4
                    emit_qk.ctr += 1
                    if j == 0:
                        wt("tensor", mV, v_h)
                    if ev_at[pb]:
                        wt("tensor", mA, ev_at[pb])
                    for kc in range(KC):
                        em("tensor", lambda e, oc=oc, kc=kc, pb=pb, bi=bi: e.matmul(
                            qk_ps[pb][:, 0:SL], wqk_sb[kc][:, P * oc:P * (oc + 1)],
                            h_buf[bi][:, kc, :],
                            start=(kc == 0), stop=(kc == KC - 1)),
                            mP if kc == KC - 1 else None)
                    p_mm = mP.v
                    if len(norm_q) >= NDEF:
                        nrm_at[pb] = emit_norm_pop()
                    wt("scalar", mP, p_mm)
                    wtt("scalar", dma_at[pb])
                    if nrm_at[pb]:
                        wt("scalar", mV, nrm_at[pb])
                    ev_at[pb] = em("scalar", lambda e, pb=pb, qb4=qb4: e.copy(
                        qkst[qb4][:], qk_ps[pb][:, 0:SL]), mA)
                    wt("gpsimd", mA, ev_at[pb])
                    if oc < KC:
                        dma_at[pb] = dma("gpsimd", qscr_r[:, oc, SL * s:SL * (s + 1)], qkst[qb4][:])
                    else:
                        dma_at[pb] = dma("gpsimd", kscr_r[:, oc - KC, SL * s:SL * (s + 1)], qkst[qb4][:])
                    wt("vector", mA, ev_at[pb])
                    if pop_mp[qb4]:
                        wt("vector", mP, pop_mp[qb4])
                    v_sq = em("vector", lambda e, qb4=qb4: e.tensor_tensor(
                        sqb[qb4][:], qkst[qb4][:], qkst[qb4][:], ALU.mult), mV)
                    if oc < KC:
                        base_i = 2 * (s * KC + oc)
                    else:
                        base_i = 64 + 2 * (s * KC + (oc - KC))
                    norm_q.append((qb4, base_i, v_sq))
            emit_qk.ctr = 0

            emit_load_cast_stats(0)
            emit_bcast(0, 0)
            v_h_cur = emit_h(0)
            for s in range(NSL):
                own = s < OSL
                n_oc = 2 * KC if own else KC
                emit_qk(s, v_h_cur, range(0, 4))
                if s + 1 < NSL:
                    emit_load_cast_stats(s + 1)
                    emit_bcast(s + 1, v_h_cur)
                    v_h_next = emit_h(s + 1)
                else:
                    v_h_next = v_h_cur
                emit_qk(s, v_h_cur, range(4, n_oc))
                v_h_cur = v_h_next
                pe_done[s] = mP.v
            while norm_q:
                emit_norm_pop()
            barrier()

            em("vector", lambda e: e.tensor_reduce(
                smallf[0:1, 0:1], qkn[0:1, 0:64], AX, ALU.max))
            dr("vector")
            em("vector", lambda e: e.tensor_reduce(
                smallf[0:1, 1:2], qkn[0:1, 64:192], AX, ALU.max), mV)
            dr("vector")
            wt("scalar", mV, mV.v)
            em("scalar", lambda e: e.activation(
                smallf[0:1, 2:3], smallf[0:1, 0:1], AF.Ln, bias=eps30[0:1, :], scale=1.0))
            dr("scalar")
            em("scalar", lambda e: e.activation(
                smallf[0:1, 3:4], smallf[0:1, 1:2], AF.Ln, bias=eps30[0:1, :], scale=1.0), mA)
            dr("scalar")
            wt("vector", mA, mA.v)
            em("vector", lambda e: e.tensor_tensor(
                smallf[0:1, 4:5], smallf[0:1, 2:3], smallf[0:1, 3:4], ALU.add), mV)
            dr("vector")
            wt("scalar", mV, mV.v)
            em("scalar", lambda e: e.activation(
                smallf[0:1, 5:6], smallf[0:1, 4:5], AF.Exp, scale=0.5), mA)
            dr("scalar")
            wt("vector", mA, mA.v)
            em("vector", lambda e: e.tensor_scalar(
                smallf[0:1, 6:7], smallf[0:1, 5:6], -0.125, None, ALU.mult), mV)
            dr("vector")
            wt("tensor", mV, mV.v)
            em("tensor", lambda e: e.matmul(
                bc_ps[:, 0:1], ones1x[:], smallf[0:1, 6:7], start=True, stop=True), mP)
            wt("vector", mP, mP.v)
            em("vector", lambda e: e.tensor_copy(negc[:], bc_ps[:, 0:1]), mV)
            dr("vector")
            barrier()

        p34 = contextlib.ExitStack()
        vres = sb("vres", [P, KT, NH * 65], BF16, p34)
        O = sb("O", [P, KC, T], BF16, p34)
        with contextlib.ExitStack() as pv:
            wv_sb = [sb(f"wv{i}", [P, D], BF16, pv) for i in range(KC)]
            wmod_lb = [sb(f"wml{i}", [P, D], F32, pv) for i in range(2)]
            mrow = sb("mrowL", [1, 2 * D], F32, pv)
            hld = [sb(f"hld{i}", [P, KC, P], BF16, pv) for i in range(2)]
            v_ps = [psum(f"v_ps{i}", P, D, F32, pv) for i in range(2)]
            mrow2_ps = psum("mrow2_ps", 1, 2 * D, F32, pv)

            em("vector", lambda e: e.memset(
                vres[:].rearrange("p t (h m) -> p t h m", m=65)[:, :, :, 64:65], 1.0), mV)
            wv_toks = [dma("gpsimd", wv_sb[kc][:],
                           wqkv_in[P * kc:P * (kc + 1), 2 * D:3 * D])
                       for kc in range(KC)]
            for tok in wv_toks:
                wtt("tensor", tok)
            v_wv = mV.v

            ml_steps = [(grp, part, kc) for grp in range(2)
                        for part in range(2) for kc in range(KC)]
            ml_done = [0, 0]
            ml_ev = [0]
            ml_rtoks = []

            def emit_mod_late(i):
                if i >= len(ml_steps):
                    return
                grp, part, kc = ml_steps[i]
                lb = i % 2
                q = ("scalar", "gpsimd")[i % 2]
                if ml_done[lb]:
                    wt(q, mP, ml_done[lb])
                if grp == 1 and part == 0 and kc == 0 and ml_ev[0]:
                    pass
                coff = 2 * D + 2048 * grp + D * part
                t_l = dma(q, wmod_lb[lb][:], wmod_in[P * kc:P * (kc + 1),
                                                     coff:coff + D])
                wtt("tensor", t_l)
                if grp == 1 and part == 0 and kc == 0:
                    wt("tensor", mA, ml_ev[0])
                for nt in range(D // 512):
                    ml_done[lb] = em("tensor", lambda e, kc=kc, nt=nt, part=part: e.matmul(
                        mrow2_ps[0:1, D * part + 512 * nt:D * part + 512 * (nt + 1)],
                        silu_c[:, kc:kc + 1],
                        wmod_lb[lb][:, 512 * nt:512 * (nt + 1)],
                        start=(kc == 0), stop=(kc == KC - 1)), mP)
                if part == 1 and kc == KC - 1:
                    wt("scalar", mP, ml_done[lb])
                    if grp == 1:
                        for tk in ml_rtoks:
                            wtt("scalar", tk)
                    ml_ev[0] = em("scalar", lambda e: e.copy(
                        mrow[0:1, :], mrow2_ps[0:1, 0:2 * D]), mA)
                    wt("gpsimd", mA, ml_ev[0])
                    for m in range(2 * KC):
                        ml_rtoks.append(dma("gpsimd",
                                            trow2[16 * grp + m:16 * grp + m + 1, :],
                                            mrow[0:1, P * m:P * (m + 1)]))
            ev_v = [0, 0]
            mm_v = [0, 0]
            for kt in range(KT):
                tb = kt % 2
                if kt >= 2:
                    wt("sync", mP, mm_v[tb])
                t_h = dma("sync", hld[tb][:], hscr_r[:, :, P * kt:P * (kt + 1)])
                wtt("tensor", t_h)
                if kt == 0:
                    wt("tensor", mV, v_wv)
                if kt >= 2:
                    wt("tensor", mV, ev_v[tb])
                for half in range(2):
                    for kc in range(KC):
                        em("tensor", lambda e, kt=kt, kc=kc, half=half, tb=tb: e.matmul(
                            v_ps[tb][:, 512 * half:512 * (half + 1)],
                            hld[tb][:, kc, :],
                            wv_sb[kc][:, 512 * half:512 * (half + 1)],
                            start=(kc == 0), stop=(kc == KC - 1)),
                            mP if (kc == KC - 1 and half == 1) else None)
                mm_v[tb] = mP.v
                emit_mod_late(kt)
                wt("vector", mP, mm_v[tb])
                ev_v[tb] = em("vector", lambda e, kt=kt, tb=tb: e.tensor_copy(
                    vres[:, kt, :].rearrange("p (h m) -> p h m", m=65)[:, :, 0:64],
                    v_ps[tb][:, 0:D].rearrange("p (h m) -> p h m", m=64)), mV)
                dr("vector")
            for tk in ml_rtoks:
                wtt("tensor", tk)
            wt("tensor", mV, ev_v[0])
            p_tp2 = em("tensor", lambda e: e.transpose(
                v_ps[0][:, 0:32], trow2[0:32, :], ident32[0:32, 0:32]), mP)
            wt("vector", mP, p_tp2)
            em("vector", lambda e: e.tensor_copy(
                modc[:, 2 * KC:6 * KC], v_ps[0][:, 0:32]), mV)
            dr("vector")
            gam = modc[:, 2 * KC:3 * KC]
            cbm = modc[:, 4 * KC:5 * KC]
            gbm = modc[:, 5 * KC:6 * KC]
            em("vector", lambda e: e.tensor_scalar(cb1[:], cbm, 1.0, None, ALU.add))
            dr("vector")
            for g_src, g_dst in ((gam, gacol), (gbm, gbcol)):
                em("vector", lambda e, s=g_src, d=g_dst: e.scalar_tensor_tensor(
                    d[:], s, -1.0, s, ALU.mult, ALU.max))
                em("vector", lambda e, s=g_src: e.tensor_scalar(
                    smallf[:, 0:KC], s, 0.0, 2.0, ALU.is_ge, ALU.mult))
                dr("vector")
                em("vector", lambda e: e.tensor_scalar(
                    smallf[:, 0:KC], smallf[:, 0:KC], -1.0, None, ALU.add))
                em("vector", lambda e, d=g_dst: e.tensor_scalar(
                    d[:], d[:], 0.25, None, ALU.max))
                dr("vector")
                em("vector", lambda e, d=g_dst: e.tensor_tensor(
                    d[:], d[:], smallf[:, 0:KC], ALU.mult), mV)
                dr("vector")
            barrier()

        xa_f = x_own[:].rearrange("p a b -> p (a b)")
        LAD.clear()
        _lad_n[0] += 1
        LAD["QF"] = 1408
        wtile = [xa_f[:, 0:1408], xa_f[:, 1408:2816]]
        LAD["ab"] = xa_f[:, 2816:4224]
        LAD["bm"] = xa_f[:, 4224:4312]
        LAD["s6"] = xa_f[:, 4312:4400]
        LAD["rc"] = xa_f[:, 4400:4488]
        xa_b = xa_f[:, 4488:16384].bitcast(BF16)
        LAD["w1"] = xa_b[:, 0:1408]
        LAD["w2"] = xa_b[:, 1408:2816]
        LAD["w3"] = xa_b[:, 2816:4224]
        LAD["w4"] = xa_b[:, 4224:5632]
        LAD["g2"] = xa_b[:, 5632:7040]
        wqt = [xa_b[:, 7040:8448], xa_b[:, 8448:9856]]
        wstage = [xa_b[:, 9856:11264], xa_b[:, 11264:12672]]
        wdts_rr = wdts.rearrange("(c p) d -> p c d", p=P)
        pw_plan = ([(wgate_in, 0, D, wgts_r, 0, t) for t in range(FI // P)]
                   + [(wup_in, 0, D, wuts_r, 0, t) for t in range(FI // P)]
                   + [(wdown_in, 1408 * hf, 1408, wdts_rr, 11 * hf, t)
                      for t in range(KC) for hf in range(2)])
        pw_i = [0]
        pw_q = [0, 0]
        pw_tT = [None, None]
        pw_st = [None, None]
        pw_pending = []

        def flush_pw_store():
            if not pw_pending:
                return
            tb, dst_r, cbase, F, t, tok = pw_pending.pop(0)
            wtt("gpsimd", tok)
            pw_st[tb] = dma("gpsimd",
                dst_r[:, cbase:cbase + F // P, P * t:P * (t + 1)],
                wstage[tb][:, 0:F].rearrange("p (c q) -> p c q", q=P))

        def emit_pw_tile():
            if pw_i[0] >= len(pw_plan):
                flush_pw_store()
                return
            src_ap, coff, F, dst_r, cbase, t = pw_plan[pw_i[0]]
            pw_i[0] += 1
            tb = pw_i[0] % 2
            if pw_q[tb]:
                wt("gpsimd", mV, pw_q[tb])
            t_l = dma("gpsimd", wtile[tb][:, 0:F],
                      src_ap[P * t:P * (t + 1), coff:coff + F])
            flush_pw_store()
            wtt("vector", t_l)
            if pw_tT[tb] is not None:
                wtt("vector", pw_tT[tb])
            for off in range(0, F, LAD["QF"]):
                w = min(LAD["QF"], F - off)
                emit_quant(wtile[tb][:, off:off + w], wqt[tb][:, off:off + w], w)
            pw_q[tb] = em("vector", lambda e: e.memset(smallf[0:1, 60:61], 0.0), mV)
            wt("sync", mV, pw_q[tb])
            wtt("sync", pw_st[tb])
            pw_tT[tb] = dmaT("sync",
                wstage[tb][:, 0:F].rearrange("p (c q) -> p c q", q=P),
                wqt[tb][:, 0:F].rearrange("p (c q) -> p c q", q=P))
            pw_pending.append((tb, dst_r, cbase, F, t, pw_tT[tb]))

        with contextlib.ExitStack() as p3:
            act_table(6)
            kbuf = [sb(f"kb{i}", [P, N], BF16, p3) for i in range(2)]
            qbuf = [sb(f"qb{i}", [P, T], BF16, p3) for i in range(2)]
            pbuf = [sb(f"pb{i}", [P, 1024], BF16, p3) for i in range(4)]
            lnsums = sb("lnsums", [P, T], F32, p3)
            NPP = NH // 2
            with contextlib.ExitStack() as p3i:
                s_ps = [psum(f"s_ps{i}", P, 1024, F32, p3i) for i in range(2)]
                pv_ps = [psum(f"pv_ps{i}", 65, 1024, F32, p3i) for i in range(2)]

                kv_done = [None, None]
                pe_kv_free = [0, 0]
                prev_exp = {}
                prev_pv = {}
                pv_free = 0
                last_ln = 0
                last_lndma = None
                NG = 2 * KT
                for pp in range(NPP):
                    pb = pp % 2
                    h0, h1 = 2 * pp, 2 * pp + 1
                    if pp == 0:
                        t1 = dma("gpsimd", kbuf[0][:], kscr[0:P, :])
                        t2 = dma("gpsimd", qbuf[0][:], qscr[0:P, :])
                        kv_done[0] = (t1, t2)
                    if pp + 1 < NPP:
                        nb = (pp + 1) % 2
                        if pp >= 1:
                            wt("gpsimd", mP, pe_kv_free[nb])
                        t1 = dma("gpsimd", kbuf[nb][:], kscr[P * (pp + 1):P * (pp + 2), :])
                        t2 = dma("gpsimd", qbuf[nb][:], qscr[P * (pp + 1):P * (pp + 2), :])
                        kv_done[nb] = (t1, t2)
                    wtt("tensor", kv_done[pb][0])
                    wtt("tensor", kv_done[pb][1])
                    for qh in range(2):
                        exp_at = {}
                        pv_at = {}
                        for g in range(NG):
                            kt, qs2 = divmod(g, 2)
                            qoff = 1024 * qh + 512 * qs2
                            sAB = s_ps[g % 2]
                            pAB = pbuf[g % 4]
                            if g >= 2:
                                wt("tensor", mA, exp_at[g - 2])
                            elif prev_exp:
                                wt("tensor", mA, prev_exp[NG - 2 + g])
                            em("tensor", lambda e, kt=kt, qoff=qoff, pb=pb, sAB=sAB: e.matmul(
                                sAB[:, 0:512], kbuf[pb][0:64, P * kt:P * (kt + 1)],
                                qbuf[pb][0:64, qoff:qoff + 512],
                                start=True, stop=True, tile_position=(0, 0)))
                            em("tensor", lambda e, kt=kt, qoff=qoff, pb=pb, sAB=sAB: e.matmul(
                                sAB[:, 512:1024], kbuf[pb][64:128, P * kt:P * (kt + 1)],
                                qbuf[pb][64:128, qoff:qoff + 512],
                                start=True, stop=True, tile_position=(64, 0)), mP)
                            sc_at = mP.v
                            if g >= 2:
                                gp = g - 2
                                ktp, qs2p = divmod(gp, 2)
                                if gp == 0 and not (pp == 0 and qh == 0):
                                    wt("tensor", mV, pv_free)
                                    wt("tensor", mA, last_ln)
                                em("tensor", lambda e, ktp=ktp, qs2p=qs2p, gp=gp, h0=h0: e.matmul(
                                    pv_ps[0][:, 512 * qs2p:512 * (qs2p + 1)],
                                    vres[:, ktp, 65 * h0:65 * (h0 + 1)],
                                    pbuf[gp % 4][:, 0:512],
                                    start=(ktp == 0), stop=(ktp == KT - 1)))
                                pv_at[gp] = em("tensor",
                                               lambda e, ktp=ktp, qs2p=qs2p, gp=gp, h1=h1: e.matmul(
                                                   pv_ps[1][:, 512 * qs2p:512 * (qs2p + 1)],
                                                   vres[:, ktp, 65 * h1:65 * (h1 + 1)],
                                                   pbuf[gp % 4][:, 512:1024],
                                                   start=(ktp == 0), stop=(ktp == KT - 1)),
                                               mP)
                            wt("scalar", mP, sc_at)
                            if g >= 3 and g % 2 == 0:
                                wt("scalar", mP, pv_at[g - 3])
                            elif g < 3 and prev_pv:
                                wt("scalar", mP, prev_pv[NG - 3 + g])
                            exp_at[g] = em("scalar", lambda e, sAB=sAB, pAB=pAB: e.activation(
                                pAB[:, 0:1024], sAB[:, 0:1024], AF.Exp,
                                bias=negc[:], scale=0.125), mA)
                        for gp in (NG - 2, NG - 1):
                            ktp, qs2p = divmod(gp, 2)
                            wt("tensor", mA, exp_at[gp])
                            em("tensor", lambda e, ktp=ktp, qs2p=qs2p, h0=h0: e.matmul(
                                pv_ps[0][:, 512 * qs2p:512 * (qs2p + 1)],
                                vres[:, ktp, 65 * h0:65 * (h0 + 1)],
                                pbuf[gp % 4][:, 0:512],
                                start=False, stop=True))
                            pv_at[gp] = em("tensor", lambda e, ktp=ktp, qs2p=qs2p, h1=h1: e.matmul(
                                pv_ps[1][:, 512 * qs2p:512 * (qs2p + 1)],
                                vres[:, ktp, 65 * h1:65 * (h1 + 1)],
                                pbuf[gp % 4][:, 512:1024],
                                start=False, stop=True), mP)
                        pv_tail = mP.v
                        pe_kv_free[pb] = pv_tail
                        wt("scalar", mP, pv_tail)
                        for hi, hh in ((0, h0), (1, h1)):
                            wtt("scalar", last_lndma)
                            em("scalar", lambda e, hi=hi: e.activation(
                                lnsums[64:65, 0:1024], pv_ps[hi][64:65, 0:1024], AF.Ln),
                                mA)
                            last_ln = mA.v
                            wt("gpsimd", mA, last_ln)
                            last_lndma = dma("gpsimd",
                                lnsums[hh:hh + 1, 1024 * qh:1024 * (qh + 1)],
                                lnsums[64:65, 0:1024])
                        wt("vector", mP, pv_tail)
                        em("vector", lambda e, h0=h0, qh=qh: e.tensor_copy(
                            O[0:64, h0 // 2, 1024 * qh:1024 * (qh + 1)],
                            pv_ps[0][0:64, 0:1024]), mV)
                        dr("vector")
                        pv_free = em("vector", lambda e, h1=h1, qh=qh: e.tensor_copy(
                            O[64:128, h1 // 2, 1024 * qh:1024 * (qh + 1)],
                            pv_ps[1][0:64, 0:1024]), mV)
                        dr("vector")
                        for _ in range(4):
                            emit_pw_tile()
                        prev_exp = exp_at
                        prev_pv = pv_at
                while pw_pending or pw_i[0] < len(pw_plan):
                    emit_pw_tile()
                for tb in (0, 1):
                    wt("sync", mV, pw_q[tb])
                    wtt("sync", pw_tT[tb])
                    wtt("sync", pw_st[tb])
                t_xr = dma("sync", x_own[:], xr[:, :, 0:T])
            with contextlib.ExitStack() as p35:
                bc_ps2 = [psum(f"bc35{i}", P, T, F32, p35) for i in range(2)]
                qwait("scalar", "gpsimd")
                a_rcp = em("scalar", lambda e: e.activation(
                    lnsums[32:48, :], lnsums[0:16, :], AF.Exp, scale=-1.0), mA)
                stg_free = 0
                tt_done = [0, 0]
                for h in range(NH):
                    b_ = 64 * (h % 2)
                    hb = h % 2
                    wt("gpsimd", mA, a_rcp)
                    if stg_free:
                        wt("gpsimd", mP, stg_free)
                    t_stg = dma("gpsimd", lnsums[0:1, :], lnsums[32 + h:33 + h, :])
                    wtt("tensor", t_stg)
                    if tt_done[hb]:
                        wt("tensor", mV, tt_done[hb])
                    for qs in range(NQS):
                        em("tensor", lambda e, qs=qs, b_=b_, hb=hb: e.matmul(
                            bc_ps2[hb][b_:b_ + 64, QS * qs:QS * (qs + 1)],
                            ones1x[:, 0:64],
                            lnsums[0:1, QS * qs:QS * (qs + 1)],
                            start=True, stop=True, tile_position=(0, b_)),
                            mP if qs == NQS - 1 else None)
                    stg_free = mP.v
                    wt("vector", mP, stg_free)
                    tt_done[hb] = em("vector", lambda e, b_=b_, h=h, hb=hb: e.tensor_tensor(
                        O[b_:b_ + 64, h // 2, 0:T],
                        O[b_:b_ + 64, h // 2, 0:T],
                        bc_ps2[hb][b_:b_ + 64, 0:T], ALU.mult), mV)
                    dr("vector")
            barrier()

        with contextlib.ExitStack() as p4:
            wpj_all = sb("wpj_all", [P, KC, D], BF16, p4)
            pj_ps = [psum(f"pj_ps{i}", P, QS, F32, p4) for i in range(2)]
            ptmp = sb("ptmp", [P, QS], F32, p4)
            wpj_toks = [dma("gpsimd", wpj_all[:, kc, :],
                            wproj_in[P * kc:P * (kc + 1), :])
                        for kc in range(KC)]
            for tok in wpj_toks:
                wtt("tensor", tok)
                dr("vector")
            wt("tensor", mV, mV.v)
            ev_pj = [0, 0]
            for oc in range(KC):
                for qs in range(NQS):
                    jj = oc * NQS + qs
                    pb = jj % 2
                    if jj >= 2:
                        wt("tensor", mV, ev_pj[pb])
                    for kc in range(KC):
                        em("tensor", lambda e, oc=oc, qs=qs, kc=kc, pb=pb: e.matmul(
                            pj_ps[pb][:, 0:QS], wpj_all[:, kc, P * oc:P * (oc + 1)],
                            O[:, kc, QS * qs:QS * (qs + 1)],
                            start=(kc == 0), stop=(kc == KC - 1)),
                            mP if kc == KC - 1 else None)
                    p_pj = mP.v
                    wt("vector", mP, p_pj)
                    em("vector", lambda e, oc=oc, pb=pb: e.tensor_scalar(
                        ptmp[:], pj_ps[pb][:, 0:QS],
                        bpcol[:, oc:oc + 1], gacol[:, oc:oc + 1], ALU.add, ALU.mult))
                    dr("vector")
                    ev_pj[pb] = em("vector", lambda e, oc=oc, qs=qs: e.tensor_tensor(
                        x_own[:, oc, QS * qs:QS * (qs + 1)],
                        x_own[:, oc, QS * qs:QS * (qs + 1)], ptmp[:], ALU.add), mV)
                    dr("vector")
            barrier()
        p34.close()

        h2q = sb("h2q", [P, KC, T], BF16)
        with contextlib.ExitStack() as p5:
            act_table(6)
            make_ladder(p5, 2048)
            x_tm = sb("x_tm", [P, NT, D], BF16, p5)
            xcb = sb("xcb", [P, T], BF16, p5)
            tpf_ps = psum("tpf5", P, 512, F32, p5)
            bcf_ps = psum("bcf5", P, 512, F32, p5)
            cbfull = sb("cbfull", [P, D], BF16, p5)
            sbfull = sb("sbfull", [P, D], BF16, p5)
            trow = sb("trow", [8, P], F32, p5)
            rrow = LAD["ab"]
            cs1 = sb("cs1", [P, NT], F32, p5)
            cs2 = sb("cs2", [P, NT], F32, p5)
            rstd2 = sb("rstd2", [P, NT], F32, p5)
            nmr2 = sb("nmr2", [P, NT], F32, p5)
            h2a = [sb(f"h2a{i}", [P, D], BF16, p5) for i in range(2)]
            h2qt = [sb(f"h2qt{i}", [P, D], BF16, p5) for i in range(2)]

            for vec, dst in ((cb1, cbfull), (sbm, sbfull)):
                wt("tensor", mV, mV.v)
                qwait("tensor", "gpsimd")
                p_tr = em("tensor", lambda e, vec=vec: e.transpose(
                    tpf_ps[0:KC, 0:P], vec[:], ident32[:]), mP)
                wt("vector", mP, p_tr)
                v_tr = em("vector", lambda e: e.tensor_copy(
                    trow[0:KC, :], tpf_ps[0:KC, 0:P]), mV)
                dr("vector")
                wt("gpsimd", mV, v_tr)
                rtoks = [dma("gpsimd", rrow[0:1, P * kc:P * (kc + 1)], trow[kc:kc + 1, :])
                         for kc in range(KC)]
                for tk in rtoks:
                    wtt("tensor", tk)
                for half in range(D // 512):
                    p_b = em("tensor", lambda e, half=half: e.matmul(
                        bcf_ps[:, 0:512], ones1x[:],
                        rrow[0:1, 512 * half:512 * (half + 1)],
                        start=True, stop=True), mP)
                    wt("vector", mP, p_b)
                    em("vector", lambda e, half=half, dst=dst: e.tensor_copy(
                        dst[:, 512 * half:512 * (half + 1)], bcf_ps[:, 0:512]), mV)
                    dr("vector")
                    if half == 0:
                        wt("tensor", mV, mV.v)
            v_mod_full = mV.v

            cdma = []
            for kc in range(KC):
                for tk in cdma:
                    wtt("vector", tk)
                v_c = em("vector", lambda e, kc=kc: e.tensor_copy(
                    xcb[:], x_own[:, kc, :]), mV)
                dr("vector")
                wt("sync", mV, v_c)
                cdma = [dmaT("sync", x_tm[:, :, P * kc:P * (kc + 1)],
                             xcb[:].rearrange("p (c q) -> p c q", q=P))]
            qwait("vector", "sync")
            qwait("vector", "scalar")
            em("vector", lambda e: e.tensor_reduce(
                cs1[:, 0:NT], x_tm[:, :, :], AX, ALU.add))
            for tp in range(NT // 2):
                for j in range(2):
                    em("vector", lambda e, j=j, tp=tp: e.tensor_tensor(
                        LAD["w1"][:, j * D:(j + 1) * D],
                        x_tm[:, 2 * tp + j, :], x_tm[:, 2 * tp + j, :], ALU.mult))
                dr("vector")
                em("vector", lambda e, tp=tp: e.tensor_reduce(
                    cs2[:, 2 * tp:2 * tp + 2],
                    LAD["w1"][:, 0:2 * D].rearrange("p (t d) -> p t d", d=D),
                    AX, ALU.add))
                dr("vector")
            em("vector", lambda e: e.tensor_scalar(
                cs1[:, 0:NT], cs1[:, 0:NT], 1.0 / D, None, ALU.mult))
            em("vector", lambda e: e.tensor_scalar(
                cs2[:, 0:NT], cs2[:, 0:NT], 1.0 / D, None, ALU.mult))
            dr("vector")
            em("vector", lambda e: e.tensor_tensor(
                rstd2[:, 0:NT], cs1[:, 0:NT], cs1[:, 0:NT], ALU.mult))
            dr("vector")
            v_v2 = em("vector", lambda e: e.tensor_tensor(
                cs2[:, 0:NT], cs2[:, 0:NT], rstd2[:, 0:NT], ALU.subtract), mV)
            wt("scalar", mV, v_v2)
            em("scalar", lambda e: e.activation(
                rstd2[:, 0:NT], cs2[:, 0:NT], AF.Ln, bias=eps5[:], scale=1.0))
            dr("scalar")
            a_r2 = em("scalar", lambda e: e.activation(
                rstd2[:, 0:NT], rstd2[:, 0:NT], AF.Exp, scale=-0.5), mA)
            wt("vector", mA, a_r2)
            em("vector", lambda e: e.scalar_tensor_tensor(
                nmr2[:, 0:NT], cs1[:, 0:NT], -1.0, rstd2[:, 0:NT],
                ALU.mult, ALU.mult), mV)
            dr("vector")
            tq_done = [[], []]
            for tp in range(NT // 2):
                tts = (2 * tp, 2 * tp + 1)
                for j, tt in enumerate(tts):
                    em("vector", lambda e, j=j, tt=tt: e.tensor_scalar(
                        h2a[j][:], x_tm[:, tt, :], rstd2[:, tt:tt + 1],
                        nmr2[:, tt:tt + 1], ALU.mult, ALU.add))
                dr("vector")
                for j in range(2):
                    em("vector", lambda e, j=j: e.tensor_tensor(
                        h2a[j][:], h2a[j][:], cbfull[:], ALU.mult))
                dr("vector")
                for j in range(2):
                    em("vector", lambda e, j=j: e.tensor_tensor(
                        h2a[j][:], h2a[j][:], sbfull[:], ALU.add))
                dr("vector")
                for tb in range(2):
                    for tk in tq_done[tb]:
                        wtt("vector", tk)
                emit_quant_pair(h2a[0][:], h2qt[0][:], D, h2a[1][:], h2qt[1][:], D)
                v_q = em("vector", lambda e: e.memset(smallf[0:1, 62:63], 0.0), mV)
                wt("sync", mV, v_q)
                for j, tt in enumerate(tts):
                    tq_done[j] = [dmaT("sync", h2q[:, :, P * tt:P * (tt + 1)],
                                       h2qt[j][:].rearrange("p (c q) -> p c q", q=P))]
            barrier()

        with contextlib.ExitStack() as p6:
            act_table(18)
            make_ladder(p6, 2048)
            hq = sb("hq", [P, FIC, TH], BF16, p6)
            wg_sb = sb("wg_sb", [P, KC, 512], BF16, p6)
            wu_sb = sb("wu_sb", [P, KC, 512], BF16, p6)
            silu_sb = [sb(f"silu_sb{i}", [P, 1024], BF16, p6) for i in range(2)]
            hmul = [sb(f"hmul{i}", [P, 1024], BF16, p6) for i in range(2)]
            wd_sb = [sb(f"wd{i}", [P, 512], BF16, p6) for i in range(4)]
            wdma = [0, 0, 0, 0]

            for th in range(2):
                tbase = TH * th
                with contextlib.ExitStack() as pga:
                    g_ps = [psum(f"g_ps{th}_{i}", P, 1024, F32, pga) for i in range(2)]
                    u_ps = [psum(f"u_ps{th}_{i}", P, 1024, F32, pga) for i in range(2)]
                    a_si = [0, 0]
                    v_mul = [0, 0]
                    tdma = [[], []]
                    pi = [0]
                    pend_q = []
                    wt("tensor", mV, mV.v)

                    def flush_q():
                        if not pend_q:
                            return
                        if len(pend_q) == 2:
                            (t0, Wa, _, _, _), (t1, Wb, _, _, _) = pend_q
                            emit_quant_pair(hmul[t0][:, 0:Wa], hmul[t0][:, 0:Wa], Wa,
                                            hmul[t1][:, 0:Wb], hmul[t1][:, 0:Wb], Wb)
                        else:
                            t0, Wa, _, _, _ = pend_q[0]
                            emit_quant(hmul[t0][:, 0:Wa], hmul[t0][:, 0:Wa], Wa)
                        v_hq = em("vector", lambda e: e.memset(
                            smallf[0:1, 61:62], 0.0), mV)
                        wt("sync", mV, v_hq)
                        for tb_, W2_, io_, iw_, qts_ in pend_q:
                            tdma[tb_] = [dmaT("sync",
                                              hq[:, io_ // P:io_ // P + iw_ // P,
                                                 P * qt:P * (qt + 1)],
                                              hmul[tb_][:, iw_ * j:iw_ * (j + 1)].rearrange(
                                                  "p (c q) -> p c q", q=P))
                                         for j, qt in enumerate(qts_)]
                        pend_q.clear()
                    for ii, (io, iw) in enumerate(isl_bounds):
                        wt("gpsimd", mP, mP.v)
                        t_g = dma("gpsimd", wg_sb[:, :, 0:iw], wgts_r[:, :, io:io + iw])
                        t_u = dma("gpsimd", wu_sb[:, :, 0:iw], wuts_r[:, :, io:io + iw])
                        wtt("tensor", t_g)
                        wtt("tensor", t_u)
                        pair = iw == 512
                        qsteps = [(qt, qt + 1) for qt in range(0, TH // P, 2)] \
                            if pair else [(qt,) for qt in range(TH // P)]
                        for qts in qsteps:
                            tb = pi[0] % 2
                            pi[0] += 1
                            W2 = iw * len(qts)
                            if a_si[tb]:
                                wt("tensor", mA, a_si[tb])
                            if v_mul[tb]:
                                wt("tensor", mV, v_mul[tb])
                            for j, qt in enumerate(qts):
                                off = 512 * j
                                for kc in range(KC):
                                    em("tensor", lambda e, kc=kc, qt=qt, iw=iw, tb=tb, tbase=tbase, off=off: e.matmul(
                                        g_ps[tb][:, off:off + iw],
                                        h2q[:, kc, tbase + P * qt:tbase + P * (qt + 1)],
                                        wg_sb[:, kc, 0:iw],
                                        start=(kc == 0), stop=(kc == KC - 1)))
                            p_gu = 0
                            for j, qt in enumerate(qts):
                                off = 512 * j
                                last_j = j == len(qts) - 1
                                for kc in range(KC):
                                    p_gu = em("tensor", lambda e, kc=kc, qt=qt, iw=iw, tb=tb, tbase=tbase, off=off: e.matmul(
                                        u_ps[tb][:, off:off + iw],
                                        h2q[:, kc, tbase + P * qt:tbase + P * (qt + 1)],
                                        wu_sb[:, kc, 0:iw],
                                        start=(kc == 0), stop=(kc == KC - 1)),
                                        mP if (kc == KC - 1 and last_j) else None)
                            WS = 512 * (len(qts) - 1) + iw
                            wt("scalar", mP, p_gu)
                            a_si[tb] = em("scalar", lambda e, WS=WS, tb=tb: e.activation(
                                silu_sb[tb][:, 0:WS], g_ps[tb][:, 0:WS], AF.Silu), mA)
                            wt("vector", mA, a_si[tb])
                            for tk in tdma[tb]:
                                wtt("vector", tk)
                            v_mul[tb] = em("vector", lambda e, WS=WS, tb=tb: e.tensor_tensor(
                                hmul[tb][:, 0:WS], silu_sb[tb][:, 0:WS], u_ps[tb][:, 0:WS],
                                ALU.mult), mV)
                            dr("vector")
                            pend_q.append((tb, W2, io, iw, qts))
                            if len(pend_q) == 2:
                                flush_q()
                    flush_q()
                    wt("tensor", mA, mA.v)
                with contextlib.ExitStack() as pdw:
                    dp_ps = [psum(f"dp_ps{th}_{i}", P, 512, F32, pdw) for i in range(8)]
                    v_res = 0
                    qwait("tensor", "sync")
                    qwait("tensor", "scalar")
                    for opp in range(2):
                        for ic in range(FIC):
                            ib = ic % 4
                            if wdma[ib]:
                                wt("gpsimd", mP, wdma[ib])
                            t_wd = dma("gpsimd", wd_sb[ib][:],
                                       wdts[P * ic:P * (ic + 1),
                                            512 * opp:512 * (opp + 1)])
                            wtt("tensor", t_wd)
                            if ic == 0:
                                wt("tensor", mV, mV.v)
                            for j in range(4):
                                for qs in range(2):
                                    wdma[ib] = em("tensor",
                                                  lambda e, ic=ic, ib=ib, j=j, qs=qs: e.matmul(
                                                      dp_ps[2 * j + qs][:, 0:512],
                                                      wd_sb[ib][:, P * j:P * (j + 1)],
                                                      hq[:, ic, 512 * qs:512 * (qs + 1)],
                                                      start=(ic == 0), stop=(ic == FIC - 1)),
                                                  mP)
                        p_d = mP.v
                        wt("vector", mP, p_d)
                        for j in range(4):
                            oc = 4 * opp + j
                            for qs in range(2):
                                v_res = em("vector",
                                           lambda e, j=j, qs=qs, oc=oc, tbase=tbase: e.scalar_tensor_tensor(
                                               x_own[:, oc, tbase + 512 * qs:tbase + 512 * (qs + 1)],
                                               dp_ps[2 * j + qs][:, 0:512],
                                               gbcol[:, oc:oc + 1],
                                               x_own[:, oc, tbase + 512 * qs:tbase + 512 * (qs + 1)],
                                               ALU.mult, ALU.add), mV)
                                dr("vector")
                    wt("sync", mV, v_res)
                    for kc in range(KC):
                        dma("sync", out_r[:, kc, tbase:tbase + TH],
                            x_own[:, kc, tbase:tbase + TH])
            qwait("sync", "sync")
            barrier()
    es.close()
    nc.finalize()
    return nc


_NC_CACHE = {}


def _get_nc():
    if "nc" not in _NC_CACHE:
        _NC_CACHE["nc"] = build(CFG)
    return _NC_CACHE["nc"]


def kernel(L0_attn_x, L0_attn_w_qkv, L0_attn_w_proj, L0_attn_b_proj,
           L0_ffn_w_gate, L0_ffn_w_up, L0_ffn_w_down, L0_c, L0_wmod):
    x = np.asarray(L0_attn_x, np.float32)
    wqkv = np.ascontiguousarray(np.asarray(L0_attn_w_qkv, np.float32))
    wproj = np.ascontiguousarray(np.asarray(L0_attn_w_proj, np.float32))
    bproj = np.asarray(L0_attn_b_proj, np.float32)
    wgate = np.ascontiguousarray(np.asarray(L0_ffn_w_gate, np.float32))
    wup = np.ascontiguousarray(np.asarray(L0_ffn_w_up, np.float32))
    wdown = np.ascontiguousarray(np.asarray(L0_ffn_w_down, np.float32))
    cmat = np.asarray(L0_c, np.float32)
    wmod = np.ascontiguousarray(np.asarray(L0_wmod, np.float32))

    N, D = CFG["N"], CFG["D"]
    T = N // 2
    KC = D // P
    ident = np.eye(P, dtype=np.float32)
    bp_col = np.ascontiguousarray(bproj.reshape(KC, P).T)

    nc = _get_nc()
    in_maps = []
    for core in range(8):
        b_, half = core // 2, core % 2
        xT = x[b_].T
        xroll = np.concatenate([xT[:, half * T:(half + 1) * T],
                                xT[:, (1 - half) * T:(2 - half) * T]], axis=1)
        c_col = np.ascontiguousarray(cmat[b_].reshape(KC, P).T)
        in_maps.append({
            "x": np.ascontiguousarray(xroll),
            "wqkv": wqkv, "wproj": wproj, "bproj": bp_col,
            "cvec": c_col, "wmod": wmod,
            "wgate": wgate, "wup": wup, "wdown": wdown,
            "ident": ident,
        })
    res = run_bass_kernel_spmd(nc, in_maps, core_ids=list(range(8)))
    out = np.empty((4, N, D), np.float32)
    for core in range(8):
        b_, half = core // 2, core % 2
        o = np.asarray(res.results[core]["out"])
        out[b_, half * T:(half + 1) * T, :] = o.T
    return out


if __name__ == "__main__":
    nc = build(CFG)
    n = sum(1 for _ in nc.all_instructions())
    print("built OK; instructions:", n)
